# version 16
# baseline (speedup 1.0000x reference)
import numpy as np
from contextlib import ExitStack

import concourse.bass as bass
import concourse.mybir as mybir
from concourse.bass_utils import run_bass_kernel_spmd

F32 = mybir.dt.float32
BF16 = mybir.dt.bfloat16
AF = mybir.ActivationFunctionType
ALU = mybir.AluOpType
AX = mybir.AxisListType

B, N, Hd = 2, 8192, 2048
NH, HD = 16, 128
BLK = 256
NCORE = 8
TPC = B * N // NCORE
NBLK = TPC // BLK
NTILE = 4
TT = TPC // NTILE
NFC = Hd // 128
EXP_I = 1408
NIC = EXP_I // 128
MOE_E = 8
NEXP = 10
SH_I = 2816
EPS_ATTN = 1e-6
EPS_LN = 1e-5
BIGIDX = 30000.0
BIGEXPN = 1.0e7
NSLOT = 3
SLOTW = 6144

_ENGS = ("PE", "ACT", "DVE", "POOL", "SP")


class Res:
    __slots__ = ("name", "w", "rs", "sem", "cnt")

    def __init__(self, name, sem=None):
        self.name = name
        self.w = None
        self.rs = []
        self.sem = sem
        self.cnt = 0


class Op:
    __slots__ = ("eng", "fn", "deps", "need", "mno", "dma_res", "dma_inc",
                 "dma_val", "waits", "snap", "idx")

    def __init__(self, eng, fn):
        self.eng = eng
        self.fn = fn
        self.idx = 0
        self.deps = []
        self.need = False
        self.mno = 0
        self.dma_res = None
        self.dma_inc = 16
        self.dma_val = 0
        self.waits = []
        self.snap = None

    def key_val(self):
        if self.dma_res is not None:
            return (self.dma_res.sem, self.dma_val)
        return ("E:" + self.eng, self.mno)


class Sched:
    def __init__(self):
        self.order = []
        self.per_eng = {e: [] for e in _ENGS}
        self.dma_sem_names = []

    def res(self, name):
        return Res(name)

    def dres(self, name):
        sem = "S:" + name
        self.dma_sem_names.append(sem)
        r = Res(name, sem=sem)
        if not hasattr(self, "dress"):
            self.dress = []
        self.dress.append(r)
        return r

    def op(self, eng, fn, reads=(), writes=(), dma_into=None, dma_inc=16):
        o = Op(eng, fn)
        best = {}
        dmas = []

        def dep(d):
            if d is None or d is o:
                return
            if d.dma_res is not None:
                if d not in dmas:
                    dmas.append(d)
                return
            if d.eng == eng and eng == "PE":
                return
            b = best.get(d.eng)
            if b is None or b.idx < d.idx:
                best[d.eng] = d

        for r in reads:
            dep(r.w)
        for r in writes:
            dep(r.w)
            for d in r.rs:
                dep(d)
        if dma_into is not None:
            dep(dma_into.w)
            for d in dma_into.rs:
                dep(d)
        o.deps = dmas + list(best.values())
        for r in reads:
            r.rs.append(o)
        for r in writes:
            r.w = o
            r.rs = []
        if dma_into is not None:
            o.dma_res = dma_into
            o.dma_inc = dma_inc
            dma_into.cnt += 1
            o.dma_val = dma_inc * dma_into.cnt
            dma_into.w = o
            dma_into.rs = []
        o.idx = len(self.per_eng[eng])
        self.order.append(o)
        self.per_eng[eng].append(o)
        return o

    def barrier(self):
        lasts = []
        for e in _ENGS:
            for d in reversed(self.per_eng[e]):
                if d.fn is not None or d.dma_res is not None:
                    lasts.append(d)
                    break
        dma_lasts = [r.w for r in getattr(self, "dress", []) if r.w is not None]
        for e in _ENGS:
            o = Op(e, None)
            for d in lasts + dma_lasts:
                if (d.eng != e or d.dma_res is not None) and d not in o.deps:
                    o.deps.append(d)
            self.order.append(o)
            self.per_eng[e].append(o)

    def finalize(self):
        for o in self.order:
            for d in o.deps:
                if d.dma_res is None:
                    d.need = True
        for e in _ENGS:
            c = 0
            for o in self.per_eng[e]:
                if o.need:
                    c += 1
                    o.mno = c
        observed = {e: {} for e in _ENGS}
        for o in self.order:
            obs = observed[o.eng]
            for d in o.deps:
                key, val = d.key_val()
                if obs.get(key, 0) >= val:
                    continue
                o.waits.append((key, val))
                if d.snap:
                    for k2, v2 in d.snap.items():
                        if obs.get(k2, 0) < v2:
                            obs[k2] = v2
                if obs.get(key, 0) < val:
                    obs[key] = val
            if o.need or o.dma_res is not None:
                snap = dict(obs)
                key, val = o.key_val()
                snap[key] = val
                o.snap = snap

    def emit(self, block, semh):
        def body_for(engname):
            ops = self.per_eng[engname]

            def body(h):
                for o in ops:
                    for (key, val) in o.waits:
                        h.wait_ge(semh[key], val)
                    if o.fn is None:
                        continue
                    inst = o.fn(h)
                    if o.dma_res is not None:
                        inst.then_inc(semh[o.dma_res.sem], o.dma_inc)
                    elif o.need:
                        inst.then_inc(semh["E:" + engname], 1)
            return body

        block.tensor(body_for("PE"))
        block.scalar(body_for("ACT"))
        block.vector(body_for("DVE"))
        block.gpsimd(body_for("POOL"))
        block.sync(body_for("SP"))


def _chunk_lhsT(w, ncols):
    return np.ascontiguousarray(w.reshape(NFC, 128, ncols).transpose(1, 0, 2))


def _chunk_rhs(w, nrow_chunks, width):
    return np.ascontiguousarray(w.reshape(nrow_chunks, 128, width).transpose(1, 0, 2))


IC_GROUPS = [(0, 2), (2, 4), (4, 6), (6, 8), (8, 10), (10, 11)]


def _build_stream(inp):
    wqkvT = np.ascontiguousarray(inp["L0_attn_w_qkv"].T)
    wgT = np.ascontiguousarray(inp["L0_attn_w_gate"].T)
    woT = np.ascontiguousarray(inp["L0_attn_w_out"].T)
    w1 = inp["L0_ffn_w1"]
    w3 = inp["L0_ffn_w3"]
    w2 = inp["L0_ffn_w2"]
    sw1 = inp["L0_ffn_sw1"]
    sw3 = inp["L0_ffn_sw3"]
    sw2 = inp["L0_ffn_sw2"]

    experts = []
    for e in range(MOE_E):
        experts.append((w1[e], w3[e], w2[e]))
    experts.append((sw1[:, :EXP_I], sw3[:, :EXP_I], sw2[:EXP_I]))
    experts.append((sw1[:, EXP_I:], sw3[:, EXP_I:], sw2[EXP_I:]))

    parts = []
    table = []
    off = [0]

    def add(kind, meta, arr2d):
        w = arr2d.shape[-1] * (arr2d.shape[1] if arr2d.ndim == 3 else 1)
        flat = arr2d.reshape(128, -1)
        assert flat.shape[1] == w and w <= SLOTW, (kind, w)
        parts.append(flat)
        table.append((kind, meta, off[0], w))
        off[0] += w

    for e, (we1, we3, we2) in enumerate(experts):
        for gi, (i0, i1) in enumerate(IC_GROUPS):
            add("w1", (e, gi), _chunk_lhsT(we1[:, i0 * 128:i1 * 128], (i1 - i0) * 128))
            add("w3", (e, gi), _chunk_lhsT(we3[:, i0 * 128:i1 * 128], (i1 - i0) * 128))
        for fb in range(4):
            add("w2", (e, fb), _chunk_rhs(we2[:, fb * 512:(fb + 1) * 512], NIC, 512))
    for h in range(NH):
        add("wq", h, _chunk_lhsT(wqkvT[:, h * 384:h * 384 + 128], 128))
    for g in range(8):
        add("wg", g, _chunk_lhsT(wgT[:, g * 256:(g + 1) * 256], 256))
    for fq in range(8):
        add("wo", fq, _chunk_rhs(woT[:, fq * 256:(fq + 1) * 256], 16, 256))

    stream = np.ascontiguousarray(np.concatenate(parts, axis=1), dtype=np.float32)
    return stream, table


def _prep(inputs):
    x = np.asarray(inputs["L0_attn_hidden_states"], dtype=np.float32)
    xf = np.ascontiguousarray(x.reshape(B * N, Hd))

    stream, table = _build_stream(inputs)

    wqkvT = np.ascontiguousarray(inputs["L0_attn_w_qkv"].T)
    wkv = np.empty((128, NH, NFC, 256), np.float32)
    for h in range(NH):
        cols = wqkvT[:, h * 384 + 128: h * 384 + 384]
        wkv[:, h] = cols.reshape(NFC, 128, 256).transpose(1, 0, 2)
    wkv = np.ascontiguousarray(wkv)

    gwT = np.ascontiguousarray(
        inputs["L0_ffn_gate_w"].T.reshape(NFC, 128, MOE_E).transpose(1, 0, 2))
    norm_col = np.ascontiguousarray(
        np.asarray(inputs["L0_attn_norm_w"], np.float32).reshape(NFC, 128).T)
    slopes = np.asarray(inputs["L0_attn_slope_rate"], np.float32).reshape(NH)
    slopes_row = np.ascontiguousarray(slopes.reshape(1, NH))

    arr_row = np.arange(1, BLK + 1, dtype=np.float32).reshape(1, BLK)
    arrK = np.empty((128, 2), np.float32)
    for j in range(2):
        for p in range(128):
            arrK[p, j] = float(BLK - (j * 128 + p + 1))
    idxT = np.empty((128, 2, BLK), np.float32)
    for j in range(2):
        for p in range(128):
            n_ = j * 128 + p
            for m in range(BLK):
                idxT[p, j, m] = float(m - n_) if m >= n_ else BIGIDX
    ident = np.eye(128, dtype=np.float32)

    shared = dict(wstream=stream, wkv=wkv, gwT=gwT, norm_col=norm_col,
                  slopes_row=slopes_row, arr_row=arr_row, arrK=arrK,
                  idxT=np.ascontiguousarray(idxT.reshape(128, 2 * BLK)),
                  ident=ident)

    in_maps = []
    for c in range(NCORE):
        cg = c % 4
        expn = np.full((1, NCORE), BIGEXPN, np.float32)
        for r in range(NCORE):
            rg = r % 4
            if (r // 4) == (c // 4) and rg < cg:
                expn[0, r] = float(BLK * NBLK * (cg - rg - 1))
        im = dict(shared)
        im["x_in"] = np.ascontiguousarray(xf[c * TPC:(c + 1) * TPC])
        im["expn"] = expn
        in_maps.append(im)
    return in_maps, table


def _build_nc(table):
    nc = bass.Bass("TRN2", target_bir_lowering=False)

    x_in = nc.dram_tensor("x_in", [TPC, Hd], F32, kind="ExternalInput")
    wstream = nc.dram_tensor("wstream", [128, table[-1][2] + table[-1][3]], F32,
                             kind="ExternalInput")
    wkv_d = nc.dram_tensor("wkv", [128, NH, NFC, 256], F32, kind="ExternalInput")
    gw_d = nc.dram_tensor("gwT", [128, NFC, MOE_E], F32, kind="ExternalInput")
    norm_d = nc.dram_tensor("norm_col", [128, NFC], F32, kind="ExternalInput")
    slopes_d = nc.dram_tensor("slopes_row", [1, NH], F32, kind="ExternalInput")
    arr_d = nc.dram_tensor("arr_row", [1, BLK], F32, kind="ExternalInput")
    arrK_d = nc.dram_tensor("arrK", [128, 2], F32, kind="ExternalInput")
    idxT_d = nc.dram_tensor("idxT", [128, 2 * BLK], F32, kind="ExternalInput")
    ident_d = nc.dram_tensor("ident", [128, 128], F32, kind="ExternalInput")
    expn_d = nc.dram_tensor("expn", [1, NCORE], F32, kind="ExternalInput")
    out_d = nc.dram_tensor("out", [TPC, Hd], F32, kind="ExternalOutput")

    xn_sc = nc.dram_tensor("xn_sc", [128, NFC, TPC], BF16)
    wstream_bf = nc.dram_tensor("wstream_bf",
                                [128, table[-1][2] + table[-1][3]], BF16)
    kvf_sc = nc.dram_tensor("kvf_sc", [128, 3, NH, TPC], BF16)
    cc_in = nc.dram_tensor("cc_in", [128, NH * HD], F32)
    cc_gather = nc.dram_tensor("cc_gather", [NCORE * 128, NH * HD], F32)

    S = Sched()

    with ExitStack() as top:
        def sb(name, shape, dt):
            return top.enter_context(nc.sbuf_tensor(name, shape, dt))

        def psum(name, shape, dt):
            return top.enter_context(nc.psum_tensor(name, shape, dt))

        pb = [psum(f"pb{i}", [128, 512], F32) for i in range(6)]
        pt2 = [psum(f"pt{i}", [128, 1024], BF16) for i in range(2)]
        pb_res = [S.res(f"pb{i}") for i in range(6)]
        pt_res2 = [S.res(f"pt{i}") for i in range(2)]
        bank_i = [0]

        def nbank():
            i = bank_i[0] % 5
            bank_i[0] += 1
            return pb[i], pb_res[i]

        ident_sb = sb("ident_sb", [128, 128], BF16)
        ident_f32 = sb("ident_f32", [128, 128], F32)
        ones_colb = sb("ones_colb", [128, 1], BF16)
        ones_rowf = sb("ones_rowf", [1, 128], F32)
        cst = sb("cst", [128, 4], F32)
        gw_sb = sb("gw_sb", [128, NFC, MOE_E], BF16)
        norm_sb = sb("norm_sb", [128, NFC], F32)
        qd_bc = sb("qd_bc", [128, NH, BLK], BF16)
        bd_bc = sb("bd_bc", [128, NH], F32)
        kd_all = sb("kd_all", [128, NH, 2], F32)
        decayT = sb("decayT", [128, NH, 2 * BLK], BF16)
        coef_bc = sb("coef_bc", [128, NH * NCORE], F32)
        kv = [sb(f"kv{i}", [128, NH, HD], F32) for i in range(2)]
        kv_bf = [sb(f"kvbf{i}", [128, NH, HD], BF16) for i in range(2)]

        r_ident = S.dres("ident")
        r_identf = S.dres("identf")
        r_ones = S.res("ones")
        r_cst = S.res("cst")
        r_gw = S.dres("gw")
        r_norm = S.dres("norm")
        r_slopes = S.dres("slopes")
        r_negsr = S.res("negsr")
        r_negsb = S.res("negsb")
        r_arr = S.dres("arr")
        r_arrK = S.dres("arrK")
        r_idxT = S.dres("idxT")
        r_expn = S.dres("expn")
        r_qdrow = S.res("qdrow")
        r_qdbc = [S.res(f"qdbc{h}") for h in range(NH)]
        r_bdrow = S.res("bdrow")
        r_bdbc = S.res("bdbc")
        r_kd = [S.res(f"kd{h}") for h in range(NH)]
        r_decayT = [S.res(f"decayT{h}") for h in range(NH)]
        r_coefrow = S.res("coefrow")
        r_coefbc = S.res("coefbc")
        r_kv = [[S.res(f"kv{i}_{h}") for h in range(NH)] for i in range(2)]
        r_kvbf = [[S.res(f"kvbf{i}_{h}") for h in range(NH)] for i in range(2)]
        r_xnsc = [S.dres(f"xnsc{b}") for b in range(NBLK)]
        r_ccin = S.dres("ccin")
        r_ccg = S.dres("ccg")
        r_out = S.dres("out")
        r_kvfsc = S.dres("kvfsc")

        with ExitStack() as pro:
            def sbp(name, shape, dt):
                return pro.enter_context(nc.sbuf_tensor(name, shape, dt))

            slopes_sb = sbp("slopes_sb", [1, NH], F32)
            negs_row = sbp("negs_row", [1, NH], F32)
            negs_bc = sbp("negs_bc", [128, NH], F32)
            arr_sb = sbp("arr_sb", [1, BLK], F32)
            arrK_sb = sbp("arrK_sb", [128, 2], F32)
            idxT_sb = sbp("idxT_sb", [128, 2 * BLK], F32)
            expn_sb = sbp("expn_sb", [1, NCORE], F32)
            qd_row = sbp("qd_row", [1, NH * BLK], F32)
            bd_row = sbp("bd_row", [1, NH], F32)
            coef_row = sbp("coef_row", [1, NH * NCORE], F32)
            dt_tmp = sbp("dt_tmp", [128, 2 * BLK], F32)
            r_dttmp = S.res("dttmp")

            S.op("POOL", lambda g: g.dma_start(out=ident_sb[:], in_=ident_d[:, :]),
                 dma_into=r_ident)
            S.op("SP", lambda s: s.dma_start(out=ident_f32[:], in_=ident_d[:, :]),
                 dma_into=r_identf)
            S.op("POOL", lambda g: g.dma_start(out=gw_sb[:], in_=gw_d[:, :, :]),
                 dma_into=r_gw)
            S.op("SP", lambda s: s.dma_start(out=norm_sb[:], in_=norm_d[:, :]),
                 dma_into=r_norm)
            S.op("SP", lambda s: s.dma_start(out=slopes_sb[:], in_=slopes_d[:, :]),
                 dma_into=r_slopes)
            S.op("SP", lambda s: s.dma_start(out=arr_sb[:], in_=arr_d[:, :]),
                 dma_into=r_arr)
            S.op("SP", lambda s: s.dma_start(out=arrK_sb[:], in_=arrK_d[:, :]),
                 dma_into=r_arrK)
            S.op("SP", lambda s: s.dma_start(out=idxT_sb[:], in_=idxT_d[:, :]),
                 dma_into=r_idxT)
            S.op("SP", lambda s: s.dma_start(out=expn_sb[:], in_=expn_d[:, :]),
                 dma_into=r_expn)
            S.op("DVE", lambda v: v.memset(ones_colb[:], 1.0), writes=[r_ones])
            S.op("DVE", lambda v: v.memset(ones_rowf[:], 1.0), writes=[r_ones])
            S.op("DVE", lambda v: v.memset(cst[:, 0:1], EPS_LN), writes=[r_cst])
            S.op("DVE", lambda v: v.memset(cst[:, 1:2], 1.0 / Hd), writes=[r_cst])
            S.op("DVE", lambda v: v.memset(cst[:, 2:3], EPS_ATTN), writes=[r_cst])
            S.op("DVE", lambda v: v.memset(cst[:, 3:4], 1.0), writes=[r_cst])

            S.op("DVE", lambda v: v.tensor_scalar(negs_row[:], slopes_sb[:], -1.0,
                                                  None, ALU.mult),
                 reads=[r_slopes], writes=[r_negsr])
            bnk, rb_ = nbank()
            S.op("PE", lambda t, b=bnk: t.matmul(b[:, 0:NH], ones_rowf[:],
                                                 negs_row[:],
                                                 start=True, stop=True),
                 reads=[r_ones, r_negsr], writes=[rb_])
            S.op("DVE", lambda v, b=bnk: v.tensor_copy(negs_bc[:], b[:, 0:NH]),
                 reads=[rb_], writes=[r_negsb])

            for h in range(NH):
                S.op("ACT", lambda s, h=h: s.activation(
                    qd_row[0:1, h * BLK:(h + 1) * BLK], arr_sb[:],
                    AF.Exp, scale=negs_row[0:1, h:h + 1]),
                    reads=[r_arr, r_negsr], writes=[r_qdrow])
            for h in range(NH):
                bnk, rb_ = nbank()
                S.op("PE", lambda t, b=bnk, h=h: t.matmul(
                    b[:, 0:BLK], ones_rowf[:], qd_row[0:1, h * BLK:(h + 1) * BLK],
                    start=True, stop=True),
                    reads=[r_ones, r_qdrow], writes=[rb_])
                S.op("DVE", lambda v, b=bnk, h=h: v.tensor_copy(qd_bc[:, h, :],
                                                                b[:, 0:BLK]),
                     reads=[rb_], writes=[r_qdbc[h]])
            S.op("DVE", lambda v: v.tensor_scalar(
                bd_row[:], slopes_sb[:], -float(BLK), None, ALU.mult),
                reads=[r_slopes], writes=[r_bdrow])
            S.op("ACT", lambda s: s.activation(bd_row[:], bd_row[:], AF.Exp),
                 reads=[r_bdrow], writes=[r_bdrow])
            bnk, rb_ = nbank()
            S.op("PE", lambda t, b=bnk: t.matmul(b[:, 0:NH], ones_rowf[:],
                                                 bd_row[:], start=True, stop=True),
                 reads=[r_ones, r_bdrow], writes=[rb_])
            S.op("DVE", lambda v, b=bnk: v.tensor_copy(bd_bc[:], b[:, 0:NH]),
                 reads=[rb_], writes=[r_bdbc])
            for h in range(NH):
                S.op("DVE", lambda v, h=h: v.tensor_scalar(
                    kd_all[:, h, :], arrK_sb[:], negs_bc[:, h:h + 1], None,
                    ALU.mult),
                    reads=[r_arrK, r_negsb], writes=[r_kd[h]])
                S.op("ACT", lambda s, h=h: s.activation(kd_all[:, h, :],
                                                        kd_all[:, h, :], AF.Exp),
                     reads=[r_kd[h]], writes=[r_kd[h]])
            for h in range(NH):
                S.op("DVE", lambda v, h=h: v.tensor_scalar(
                    dt_tmp[:], idxT_sb[:], negs_bc[:, h:h + 1], None, ALU.mult),
                    reads=[r_idxT, r_negsb], writes=[r_dttmp])
                S.op("ACT", lambda s, h=h: s.activation(decayT[:, h, :], dt_tmp[:],
                                                        AF.Exp),
                     reads=[r_dttmp], writes=[r_decayT[h]])
            for h in range(NH):
                S.op("DVE", lambda v, h=h: v.tensor_scalar(
                    coef_row[0:1, h * NCORE:(h + 1) * NCORE], expn_sb[:],
                    negs_row[0:1, h:h + 1], None, ALU.mult),
                    reads=[r_expn, r_negsr], writes=[r_coefrow])
            S.op("ACT", lambda s: s.activation(coef_row[:], coef_row[:], AF.Exp),
                 reads=[r_coefrow], writes=[r_coefrow])
            bnk, rb_ = nbank()
            S.op("PE", lambda t, b=bnk: t.matmul(b[:, 0:NH * NCORE], ones_rowf[:],
                                                 coef_row[:], start=True, stop=True),
                 reads=[r_ones, r_coefrow], writes=[rb_])
            S.op("DVE", lambda v, b=bnk: v.tensor_copy(coef_bc[:],
                                                       b[:, 0:NH * NCORE]),
                 reads=[rb_], writes=[r_coefbc])

            for h in range(NH):
                S.op("DVE", lambda v, h=h: v.memset(kv[0][:, h, :], 0.0),
                     writes=[r_kv[0][h]])

        S.barrier()

        with ExitStack() as p0:
            def sb0(name, shape, dt):
                return p0.enter_context(nc.sbuf_tensor(name, shape, dt))

            x_tc = [sb0(f"x_tc{i}", [128, Hd], F32) for i in range(3)]
            xn_tm = [sb0(f"xn_tm{i}", [128, Hd], F32) for i in range(2)]
            sq_b = xn_tm
            st_b = [sb0(f"st{i}", [128, 6], F32) for i in range(2)]
            xn_all = sb0("xn_all", [128, NFC, TPC], BF16)

            r_xtc = [S.dres(f"xtc{i}") for i in range(3)]
            r_xntm = [S.res(f"xntm{i}") for i in range(2)]
            r_sq = r_xntm
            r_st = [S.res(f"st{i}") for i in range(2)]
            r_xnall = [S.res(f"xnall{b}") for b in range(NBLK)]

            def ln_chain(tc):
                bi = tc % 2
                xt, rxt = x_tc[tc % 3], r_xtc[tc % 3]
                sq, rsq = sq_b[bi], r_sq[bi]
                xm, rxm = xn_tm[bi], r_xntm[bi]
                stt, rst = st_b[bi], r_st[bi]
                S.op("SP", lambda s, xt=xt, tc=tc: s.dma_start(
                    out=xt[:], in_=x_in[tc * 128:(tc + 1) * 128, :]),
                    dma_into=rxt)
                S.op("DVE", lambda v, xt=xt, stt=stt: v.reduce_sum(
                    stt[:, 0:1], xt[:], axis=AX.X),
                    reads=[rxt], writes=[rst])
                S.op("DVE", lambda v, stt=stt: v.tensor_scalar(
                    stt[:, 1:2], stt[:, 0:1], -1.0 / Hd, None, ALU.mult),
                    reads=[rst], writes=[rst])
                S.op("ACT", lambda s, xt=xt, sq=sq, stt=stt: s.activation(
                    sq[:], xt[:], AF.Square, bias=stt[:, 1:2]),
                    reads=[rxt, rst], writes=[rsq])
                S.op("DVE", lambda v, sq=sq, stt=stt: v.reduce_sum(
                    stt[:, 2:3], sq[:], axis=AX.X),
                    reads=[rsq], writes=[rst])
                S.op("ACT", lambda s, stt=stt: s.activation(
                    stt[:, 3:4], stt[:, 2:3], AF.Sqrt,
                    scale=cst[:, 1:2], bias=cst[:, 0:1]),
                    reads=[rst, r_cst], writes=[rst])
                S.op("DVE", lambda v, stt=stt: v.reciprocal(
                    stt[:, 4:5], stt[:, 3:4]),
                    reads=[rst], writes=[rst])
                S.op("DVE", lambda v, stt=stt: v.tensor_tensor(
                    stt[:, 5:6], stt[:, 1:2], stt[:, 4:5], ALU.mult),
                    reads=[rst], writes=[rst])
                S.op("DVE", lambda v, xt=xt, xm=xm, stt=stt: v.tensor_scalar(
                    xm[:], xt[:], stt[:, 4:5], stt[:, 5:6],
                    ALU.mult, ALU.add),
                    reads=[rxt, rst], writes=[rxm])

            def ln_stage2(tc):
                bi = tc % 2
                xm, rxm = xn_tm[bi], r_xntm[bi]
                rblk = r_xnall[tc // 2]
                c0, c1 = tc * 128, (tc + 1) * 128
                for q4 in range(4):
                    bnk, rbk = nbank()
                    for j in range(4):
                        fc = q4 * 4 + j
                        S.op("PE", lambda t, xm=xm, fc=fc, j=j, b=bnk: t.transpose(
                            b[:, j * 128:(j + 1) * 128],
                            xm[:, fc * 128:(fc + 1) * 128], ident_f32[:]),
                            reads=[rxm, r_identf], writes=[rbk])
                    for j in range(4):
                        fc = q4 * 4 + j
                        S.op("DVE", lambda v, fc=fc, j=j, b=bnk, c0=c0, c1=c1:
                             v.tensor_copy(
                                 xn_all[:, fc, c0:c1], b[:, j * 128:(j + 1) * 128]),
                             reads=[rbk], writes=[rblk])
                S.op("SP", lambda s, tc=tc, c0=c0, c1=c1: s.dma_start(
                    out=xn_sc[:, :, c0:c1], in_=xn_all[:, :, c0:c1]),
                    reads=[rblk], dma_into=r_xnsc[tc // 2])

            NTC = TPC // 128

            HG = 4
            wkvg = sb0("wkvg", [128, HG, NFC, 256], BF16)
            kb_b = [sb0(f"kb{i}", [128, BLK], BF16) for i in range(3)]
            vb_b = [sb0(f"vb{i}", [128, BLK], BF16) for i in range(3)]
            ktm_b = [sb0(f"ktm{i}", [128, BLK], BF16) for i in range(2)]
            vtm_b = [sb0(f"vtm{i}", [128, BLK], BF16) for i in range(2)]

            r_wkvgh = [S.dres("wkvgA"), S.dres("wkvgB")]
            r_kb = [S.res(f"kb{i}") for i in range(3)]
            r_vb = [S.res(f"vb{i}") for i in range(3)]
            r_ktm = [S.res(f"ktm{i}") for i in range(2)]
            r_vtm = [S.res(f"vtm{i}") for i in range(2)]

            def p1_proj(blk, hh, si):
                rxb = r_xnall[blk]
                rwk = r_wkvgh[hh // 2]
                m0, m1 = blk * BLK, (blk + 1) * BLK
                par3 = si % 3
                kb, rkb = kb_b[par3], r_kb[par3]
                vb, rvb = vb_b[par3], r_vb[par3]
                bnk, rbk = nbank()
                for fc in range(NFC):
                    S.op("PE", lambda t, b=bnk, hh=hh, fc=fc, m0=m0, m1=m1:
                         t.matmul(b[:, 0:BLK], wkvg[:, hh, fc, 0:128],
                                  xn_all[:, fc, m0:m1],
                                  start=(fc == 0), stop=(fc == NFC - 1)),
                         reads=[rwk, rxb], writes=[rbk])
                S.op("ACT", lambda s, b=bnk, kb=kb: s.activation(
                    kb[:], b[:, 0:BLK], AF.Silu),
                    reads=[rbk], writes=[rkb])
                bnk2, rbk2 = nbank()
                for fc in range(NFC):
                    S.op("PE", lambda t, b=bnk2, hh=hh, fc=fc, m0=m0, m1=m1:
                         t.matmul(b[:, 0:BLK], wkvg[:, hh, fc, 128:256],
                                  xn_all[:, fc, m0:m1],
                                  start=(fc == 0), stop=(fc == NFC - 1)),
                         reads=[rwk, rxb], writes=[rbk2])
                S.op("ACT", lambda s, b=bnk2, vb=vb: s.activation(
                    vb[:], b[:, 0:BLK], AF.Silu),
                    reads=[rbk2], writes=[rvb])

            def p1_T(blk, hh, si):
                par3 = si % 3
                par = si % 2
                kb, rkb = kb_b[par3], r_kb[par3]
                vb, rvb = vb_b[par3], r_vb[par3]
                ptb, rptb = pt2[par], pt_res2[par]
                for j in range(2):
                    S.op("PE", lambda t, j=j, kb=kb, ptb=ptb: t.transpose(
                        ptb[:, j * 128:(j + 1) * 128],
                        kb[:, j * 128:(j + 1) * 128], ident_sb[:]),
                        reads=[rkb, r_ident], writes=[rptb])
                    S.op("PE", lambda t, j=j, vb=vb, ptb=ptb: t.transpose(
                        ptb[:, (2 + j) * 128:(3 + j) * 128],
                        vb[:, j * 128:(j + 1) * 128], ident_sb[:]),
                        reads=[rvb, r_ident], writes=[rptb])

            def p1_A(blk, hh, si, hg):
                h = hg * HG + hh
                par3 = si % 3
                par = si % 2
                kb, rkb = kb_b[par3], r_kb[par3]
                ktm, rktm = ktm_b[par], r_ktm[par]
                vtm, rvtm = vtm_b[par], r_vtm[par]
                ptb, rptb = pt2[par], pt_res2[par]
                cur, nxt = blk % 2, (blk + 1) % 2
                for j in range(2):
                    S.op("DVE", lambda v, h=h, j=j, ktm=ktm, ptb=ptb:
                         v.tensor_scalar(
                             ktm[:, j * 128:(j + 1) * 128],
                             ptb[:, j * 128:(j + 1) * 128],
                             kd_all[:, h, j:j + 1], None, ALU.mult),
                         reads=[rptb, r_kd[h]], writes=[rktm])
                    S.op("DVE", lambda v, j=j, vtm=vtm, ptb=ptb:
                         v.tensor_copy(
                             vtm[:, j * 128:(j + 1) * 128],
                             ptb[:, (2 + j) * 128:(3 + j) * 128]),
                         reads=[rptb], writes=[rvtm])
                bnk3, rbk3 = nbank()
                for j in range(2):
                    S.op("PE", lambda t, b=bnk3, j=j, ktm=ktm, vtm=vtm:
                         t.matmul(
                             b[:, 0:HD], ktm[:, j * 128:(j + 1) * 128],
                             vtm[:, j * 128:(j + 1) * 128],
                             start=(j == 0), stop=(j == 1)),
                         reads=[rktm, rvtm], writes=[rbk3])
                S.op("DVE", lambda v, b=bnk3, h=h, cur=cur, nxt=nxt:
                     v.scalar_tensor_tensor(
                         kv[nxt][:, h, :], kv[cur][:, h, :],
                         bd_bc[:, h:h + 1], b[:, 0:HD],
                         ALU.mult, ALU.add),
                     reads=[r_kv[cur][h], r_bdbc, rbk3],
                     writes=[r_kv[nxt][h]])
                for kind, buf, rbuf in ((0, kb, rkb), (1, ktm, rktm),
                                        (2, vtm, rvtm)):
                    S.op("SP", lambda s, kind=kind, buf=buf, h=h, blk=blk:
                         s.dma_start(
                             out=kvf_sc[:, kind, h, blk * BLK:(blk + 1) * BLK],
                             in_=buf[:]),
                         reads=[rbuf], dma_into=r_kvfsc)

            pend = []
            si_box = [0]

            def scan_step(blk, hh, hg):
                si = si_box[0]
                p1_proj(blk, hh, si)
                pend.append((blk, hh, si, hg))
                if len(pend) >= 2:
                    p1_T(pend[-2][0], pend[-2][1], pend[-2][2])
                if len(pend) >= 3:
                    b3, h3, s3, g3 = pend.pop(0)
                    p1_A(b3, h3, s3, g3)
                si_box[0] += 1

            def scan_flush():
                if pend:
                    p1_T(pend[-1][0], pend[-1][1], pend[-1][2])
                while pend:
                    b3, h3, s3, g3 = pend.pop(0)
                    p1_A(b3, h3, s3, g3)

            def wkvg_load(hg, half):
                h0 = hg * HG + half * 2
                S.op("POOL", lambda g, h0=h0, half=half: g.dma_start(
                    out=wkvg[:, half * 2:half * 2 + 2, :, :],
                    in_=wkv_d[:, h0:h0 + 2, :, :]),
                    dma_into=r_wkvgh[half])

            def after_step(blk, hh, hg):
                if blk == NBLK - 1 and hh % 2 == 1 and hg + 1 < NH // HG:
                    wkvg_load(hg + 1, hh // 2)

            wkvg_load(0, 0)
            wkvg_load(0, 1)
            for tc in range(NTC):
                ln_chain(tc)
                if tc >= 1:
                    ln_stage2(tc - 1)
                    if (tc - 1) % 2 == 1:
                        blk = (tc - 1) // 2
                        for hh in range(HG):
                            scan_step(blk, hh, 0)
                            after_step(blk, hh, 0)
            ln_stage2(NTC - 1)
            for hh in range(HG):
                scan_step(NBLK - 1, hh, 0)
                after_step(NBLK - 1, hh, 0)

            for hg in range(1, NH // HG):
                for blk in range(NBLK):
                    for hh in range(HG):
                        scan_step(blk, hh, hg)
                        after_step(blk, hh, hg)
            scan_flush()

        S.barrier()
        slot = [sb(f"slot{i}", [128, SLOTW], BF16) for i in range(NSLOT)]
        r_slot = [S.dres(f"slot{i}") for i in range(NSLOT)]
        r_wsbf = S.dres("wsbf")
        sall = sb("sall", [128, NH * HD], F32)
        r_sall = S.dres("sall")

        def emit_p2a():
            S.op("SP", lambda s: s.dma_start(
                out=cc_in[:, :].rearrange("p (h d) -> p h d", h=NH),
                in_=kv[0][:, :, :]),
                reads=[r_kv[0][h] for h in range(NH)], dma_into=r_ccin)
            S.op("POOL", lambda g: g.collective_compute(
                "AllGather", ALU.bypass, replica_groups=[list(range(NCORE))],
                ins=[cc_in.ap().opt()], outs=[cc_gather.ap().opt()]),
                reads=[r_ccin], dma_into=r_ccg, dma_inc=1)
            for h in range(NH):
                S.op("DVE", lambda v, h=h: v.memset(kv[0][:, h, :], 0.0),
                     writes=[r_kv[0][h]])

        def emit_p2b():
            for r in range(NCORE):
                S.op("SP", lambda s, r=r: s.dma_start(
                    out=sall[:], in_=cc_gather[r * 128:(r + 1) * 128, :]),
                    reads=[r_ccg], dma_into=r_sall)
                cur, nxt = r % 2, (r + 1) % 2
                for h in range(NH):
                    S.op("DVE", lambda v, r=r, h=h, cur=cur, nxt=nxt:
                         v.scalar_tensor_tensor(
                             kv[nxt][:, h, :], sall[:, h * HD:(h + 1) * HD],
                             coef_bc[:, h * NCORE + r: h * NCORE + r + 1],
                             kv[cur][:, h, :], ALU.mult, ALU.add),
                         reads=[r_sall, r_coefbc, r_kv[cur][h]],
                         writes=[r_kv[nxt][h]])
            for h in range(NH):
                S.op("ACT", lambda s, h=h: s.copy(kv_bf[0][:, h, :], kv[0][:, h, :]),
                     reads=[r_kv[0][h]], writes=[r_kvbf[0][h]])

        with ExitStack() as p3:
            def sb3(name, shape, dt):
                return p3.enter_context(nc.sbuf_tensor(name, shape, dt))

            xn_sl2 = [sb3(f"xn_sl{i}", [128, NFC, TT], BF16) for i in range(2)]
            acc = sb3("acc", [128, 4, Hd], F32)
            o_fm = sb3("o_fm", [128, NFC, TT], BF16)
            qf2 = [sb3(f"qf{i}", [128, TT], BF16) for i in range(2)]
            kvf_b = [sb3(f"kvf{i}", [128, 3, TT], BF16) for i in range(2)]
            kf2 = [kvf_b[i][:, 0, :] for i in range(2)]
            ktm32 = [kvf_b[i][:, 1, :] for i in range(2)]
            vtm32 = [kvf_b[i][:, 2, :] for i in range(2)]
            qkd2 = [sb3(f"qkd{i}", [128, 2, BLK], BF16) for i in range(2)]
            t12 = [sb3(f"t1{i}", [128, BLK], F32) for i in range(2)]
            osq = [sb3(f"osq{i}", [128, TT], BF16) for i in range(2)]
            g_tmp = sb3("g_tmp", [128, TT], BF16)
            t2 = sb3("t2", [128, TT], BF16)
            var_row = sb3("var_row", [1, TT], F32)
            rstd_bc = sb3("rstd_bc", [128, TT], BF16)
            ce = sb3("ce", [128, 4, MOE_E], F32)
            lg = sb3("lg", [128, 64], F32)
            tfm = [sb3(f"tfm{i}", [128, TT], BF16) for i in range(NIC)]
            hsb = sb3("hsb", [128, TT], BF16)

            r_xnsl = [S.dres(f"xnsl{i}") for i in range(2)]
            r_acc = [S.dres(f"acc{mc}") for mc in range(4)]
            r_ofm = [S.res(f"ofm{oc}") for oc in range(NFC)]
            r_qf = [S.res(f"qf{i}") for i in range(2)]
            r_kvf = [S.dres(f"kvf{i}") for i in range(2)]
            r_kf = r_kvf
            r_ktm3 = r_kvf
            r_vtm3 = r_kvf
            r_qkd = [S.res(f"qkd{i}") for i in range(2)]
            r_t1 = [S.res(f"t1{i}") for i in range(2)]
            r_osq = [S.res(f"osq{i}") for i in range(2)]
            r_gtmp = S.res("gtmp")
            r_t2 = S.res("t2")
            r_var = S.res("var")
            r_rstd = S.res("rstd")
            r_ce = [S.res(f"ce{mc}") for mc in range(4)]
            r_lg = S.res("lg")
            r_tfm = [S.res(f"tfm{i}") for i in range(NIC)]
            r_hsb = S.res("hsb")

            nxc = NEXP * (len(IC_GROUPS) * 2 + 4)
            table3 = table[nxc:] + table[:nxc]
            chunks_flat = table * (NTILE - 1) + table3
            emitted = [0]
            consumed = [0]
            ntab = len(table)

            def ensure_chunks(upto):
                while emitted[0] <= upto:
                    ci = emitted[0]
                    _, _, off, w = chunks_flat[ci]
                    i = ci % NSLOT
                    if ci < ntab:
                        S.op("POOL", lambda g, i=i, off=off, w=w: g.dma_start(
                            out=slot[i][:, 0:w], in_=wstream[:, off:off + w]),
                            dma_into=r_slot[i])
                        S.op("SP", lambda s, i=i, off=off, w=w: s.dma_start(
                            out=wstream_bf[:, off:off + w], in_=slot[i][:, 0:w]),
                            reads=[r_slot[i]], dma_into=r_wsbf)
                    else:
                        S.op("POOL", lambda g, i=i, off=off, w=w: g.dma_start(
                            out=slot[i][:, 0:w], in_=wstream_bf[:, off:off + w]),
                            reads=[r_wsbf], dma_into=r_slot[i])
                    emitted[0] += 1

            def next_slot(kind, meta, off, w):
                ci = consumed[0]
                ck, cm, co, cw = chunks_flat[ci]
                assert (ck, cm, co, cw) == (kind, meta, off, w), (ci, ck, kind)
                ensure_chunks(ci)
                consumed[0] += 1
                i = ci % NSLOT
                return slot[i], r_slot[i]

            pending_xout = [None]

            def emit_xout(t0x):
                for mc in range(4):
                    S.op("SP", lambda s, t0x=t0x, mc=mc: s.dma_start(
                        out=sall[:],
                        in_=x_in[t0x + mc * 128: t0x + (mc + 1) * 128, :]),
                        dma_into=r_sall)
                    S.op("DVE", lambda v, mc=mc: v.tensor_tensor(
                        acc[:, mc, :], acc[:, mc, :], sall[:], ALU.add),
                        reads=[r_sall, r_acc[mc]], writes=[r_acc[mc]])
                    S.op("SP", lambda s, t0x=t0x, mc=mc: s.dma_start(
                        out=out_d[t0x + mc * 128: t0x + (mc + 1) * 128, :],
                        in_=acc[:, mc, :]),
                        reads=[r_acc[mc]], dma_into=r_out)

            for tile in range(NTILE):
                t0 = tile * TT
                cur0 = 0
                xn_sl = xn_sl2[tile % 2]
                r_xnsl_t = r_xnsl[tile % 2]
                attn_first = (tile == NTILE - 1)
                ttab = table3 if attn_first else table

                S.op("SP", lambda s, t0=t0, xn_sl=xn_sl: s.dma_start(
                    out=xn_sl[:], in_=xn_sc[:, :, t0:t0 + TT]),
                    reads=[r_xnsc[2 * tile], r_xnsc[2 * tile + 1]],
                    dma_into=r_xnsl_t)

                tib = [0]

                def sec_logits():
                    for mc in range(4):
                        bnk, rbk = nbank()
                        for fc in range(NFC):
                            S.op("PE", lambda t, b=bnk, fc=fc, mc=mc, xn_sl=xn_sl:
                                 t.matmul(
                                     b[:, 0:MOE_E],
                                     xn_sl[:, fc, mc * 128:(mc + 1) * 128],
                                     gw_sb[:, fc, :],
                                     start=(fc == 0), stop=(fc == NFC - 1)),
                                 reads=[r_xnsl_t, r_gw], writes=[rbk])
                        E8 = MOE_E
                        c0 = mc * E8
                        S.op("DVE", lambda v, b=bnk: v.reduce_max(
                            lg[:, 32:33], b[:, 0:E8], axis=AX.X),
                            reads=[rbk], writes=[r_lg])
                        S.op("DVE", lambda v, b=bnk, c0=c0: v.tensor_scalar(
                            lg[:, c0:c0 + E8], b[:, 0:E8], lg[:, 32:33], None,
                            ALU.subtract),
                            reads=[rbk, r_lg], writes=[r_lg])
                        S.op("DVE", lambda v, c0=c0, mc=mc: v.tensor_scalar(
                            ce[:, mc, :], lg[:, c0:c0 + E8], 0.0, None, ALU.is_ge),
                            reads=[r_lg], writes=[r_ce[mc]])
                        S.op("DVE", lambda v, c0=c0, mc=mc: v.scalar_tensor_tensor(
                            lg[:, 34:34 + E8], ce[:, mc, :], -1.0e9,
                            lg[:, c0:c0 + E8], ALU.mult, ALU.add),
                            reads=[r_ce[mc], r_lg], writes=[r_lg])
                        S.op("DVE", lambda v: v.reduce_max(
                            lg[:, 42:43], lg[:, 34:34 + MOE_E], axis=AX.X),
                            reads=[r_lg], writes=[r_lg])
                        S.op("ACT", lambda s, c0=c0: s.activation(
                            lg[:, 43:43 + MOE_E], lg[:, c0:c0 + MOE_E], AF.Exp),
                            reads=[r_lg], writes=[r_lg])
                        S.op("ACT", lambda s: s.activation(
                            lg[:, 51:52], lg[:, 42:43], AF.Exp),
                            reads=[r_lg], writes=[r_lg])
                        S.op("DVE", lambda v: v.tensor_scalar(
                            lg[:, 52:53], lg[:, 51:52], 1.0, None, ALU.add),
                            reads=[r_lg], writes=[r_lg])
                        S.op("DVE", lambda v: v.reciprocal(
                            lg[:, 53:54], lg[:, 52:53]),
                            reads=[r_lg], writes=[r_lg])
                        S.op("DVE", lambda v, c0=c0: v.tensor_scalar(
                            lg[:, 54:54 + MOE_E], lg[:, c0:c0 + MOE_E],
                            lg[:, 42:43], None, ALU.is_ge),
                            reads=[r_lg], writes=[r_lg])
                        S.op("DVE", lambda v: v.tensor_tensor(
                            lg[:, 43:43 + MOE_E], lg[:, 43:43 + MOE_E],
                            lg[:, 54:54 + MOE_E], ALU.mult),
                            reads=[r_lg], writes=[r_lg])
                        S.op("DVE", lambda v, mc=mc: v.tensor_scalar(
                            ce[:, mc, :], lg[:, 43:43 + MOE_E], lg[:, 53:54],
                            None, ALU.mult),
                            reads=[r_lg], writes=[r_ce[mc]])

                def sec_experts():
                    for e in range(NEXP):
                        if tile == 0 and e == 1:
                            emit_p2a()
                        if tile == 0 and e == 3:
                            emit_p2b()
                        for gi, (i0, i1) in enumerate(IC_GROUPS):
                            kind, meta, off, w = ttab[tib[0]]; tib[0] += 1
                            assert kind == "w1" and meta == (e, gi)
                            sl1, rsl1 = next_slot(kind, meta, off, w)
                            sv1 = sl1[:, 0:w].rearrange("p (c w) -> p c w", c=NFC)
                            kind, meta, off, w = ttab[tib[0]]; tib[0] += 1
                            assert kind == "w3" and meta == (e, gi)
                            sl3, rsl3 = next_slot(kind, meta, off, w)
                            sv3 = sl3[:, 0:w].rearrange("p (c w) -> p c w", c=NFC)
                            for ic in range(i0, i1):
                                jj = ic - i0
                                bnh, rbh = nbank()
                                for fc in range(NFC):
                                    S.op("PE", lambda t, b=bnh, fc=fc, jj=jj, sv1=sv1,
                                         xn_sl=xn_sl:
                                         t.matmul(b[:], sv1[:, fc, jj * 128:(jj + 1) * 128],
                                                  xn_sl[:, fc, :],
                                                  start=(fc == 0), stop=(fc == NFC - 1)),
                                         reads=[rsl1, r_xnsl_t], writes=[rbh])
                                S.op("ACT", lambda s, b=bnh: s.activation(
                                    hsb[:], b[:], AF.Silu),
                                    reads=[rbh], writes=[r_hsb])
                                bng, rbg = nbank()
                                for fc in range(NFC):
                                    S.op("PE", lambda t, b=bng, fc=fc, jj=jj, sv3=sv3,
                                         xn_sl=xn_sl:
                                         t.matmul(b[:], sv3[:, fc, jj * 128:(jj + 1) * 128],
                                                  xn_sl[:, fc, :],
                                                  start=(fc == 0), stop=(fc == NFC - 1)),
                                         reads=[rsl3, r_xnsl_t], writes=[rbg])
                                S.op("DVE", lambda v, b=bng, ic=ic: v.tensor_tensor(
                                    tfm[ic][:], hsb[:], b[:], ALU.mult),
                                    reads=[r_hsb, rbg], writes=[r_tfm[ic]])
                        for fb in range(4):
                            kind, meta, off, w = ttab[tib[0]]; tib[0] += 1
                            assert kind == "w2" and meta == (e, fb)
                            sl2, rsl2 = next_slot(kind, meta, off, w)
                            sv2 = sl2[:, 0:w].rearrange("p (c w) -> p c w", c=NIC)
                            for mc in range(4):
                                bny, rby = nbank()
                                for ic in range(NIC):
                                    S.op("PE", lambda t, b=bny, ic=ic, mc=mc, sv2=sv2:
                                         t.matmul(b[:],
                                                  tfm[ic][:, mc * 128:(mc + 1) * 128],
                                                  sv2[:, ic, :],
                                                  start=(ic == 0), stop=(ic == NIC - 1)),
                                         reads=[r_tfm[ic], rsl2], writes=[rby])
                                if e == 0 and not attn_first:
                                    S.op("DVE", lambda v, b=bny, mc=mc, fb=fb:
                                         v.tensor_scalar(
                                             acc[:, mc, fb * 512:(fb + 1) * 512],
                                             b[:], ce[:, mc, 0:1], None, ALU.mult),
                                         reads=[rby, r_ce[mc]], writes=[r_acc[mc]])
                                elif e < MOE_E:
                                    S.op("DVE", lambda v, b=bny, mc=mc, fb=fb, e=e:
                                         v.scalar_tensor_tensor(
                                             acc[:, mc, fb * 512:(fb + 1) * 512],
                                             b[:], ce[:, mc, e:e + 1],
                                             acc[:, mc, fb * 512:(fb + 1) * 512],
                                             ALU.mult, ALU.add),
                                         reads=[rby, r_ce[mc], r_acc[mc]],
                                         writes=[r_acc[mc]])
                                else:
                                    S.op("DVE", lambda v, b=bny, mc=mc, fb=fb:
                                         v.tensor_tensor(
                                             acc[:, mc, fb * 512:(fb + 1) * 512],
                                             acc[:, mc, fb * 512:(fb + 1) * 512],
                                             b[:], ALU.add),
                                         reads=[rby, r_acc[mc]], writes=[r_acc[mc]])

                def sec_attn():
                    def at_proj(h):
                        kind, meta, off, w = ttab[tib[0]]; tib[0] += 1
                        assert kind == "wq" and meta == h
                        sl, rsl = next_slot(kind, meta, off, w)
                        slv = sl[:, 0:w].rearrange("p (c w) -> p c w", c=NFC)
                        par = h % 2
                        S.op("SP", lambda s, par=par, h=h, t0=t0: s.dma_start(
                            out=kvf_b[par][:, :, :],
                            in_=kvf_sc[:, :, h, t0:t0 + TT]),
                            reads=[r_kvfsc], dma_into=r_kvf[par])
                        dst, rdst = qf2[par], r_qf[par]
                        bnk, rbk = nbank()
                        for fc in range(NFC):
                            S.op("PE", lambda t, b=bnk, fc=fc, slv=slv,
                                 xn_sl=xn_sl:
                                 t.matmul(b[:], slv[:, fc, 0:128],
                                          xn_sl[:, fc, :],
                                          start=(fc == 0), stop=(fc == NFC - 1)),
                                 reads=[rsl, r_xnsl_t], writes=[rbk])
                        S.op("ACT", lambda s, b=bnk, dst=dst: s.activation(
                            dst[:], b[:], AF.Silu),
                            reads=[rbk], writes=[rdst])

                    def at_tail(h):
                        par = h % 2
                        qf, kf = qf2[par], kf2[par]
                        ktm3, vtm3 = ktm32[par], vtm32[par]
                        qkd, t1 = qkd2[par], t12[par]
                        rqf, rkf = r_qf[par], r_kf[par]
                        rktm3, rvtm3 = r_ktm3[par], r_vtm3[par]
                        rqkd, rt1 = r_qkd[par], r_t1[par]
                        for b_ in range(2):
                            cur = b_ % 2
                            nxt = 1 - cur
                            m0 = b_ * BLK
                            bni, rbi = nbank()
                            S.op("PE", lambda t, b=bni, h=h, cur=cur, m0=m0, qf=qf:
                                 t.matmul(
                                     b[:, 0:BLK], kv_bf[cur][:, h, :],
                                     qf[:, m0:m0 + BLK], start=True, stop=True),
                                 reads=[r_kvbf[cur][h], rqf], writes=[rbi])
                            for nch in range(2):
                                bnq, rbq = nbank()
                                S.op("PE", lambda t, b=bnq, nch=nch, m0=m0, kf=kf,
                                     qf=qf: t.matmul(
                                         b[:, 0:BLK],
                                         kf[:, m0 + nch * 128:m0 + (nch + 1) * 128],
                                         qf[:, m0:m0 + BLK], start=True, stop=True),
                                     reads=[rkf, rqf], writes=[rbq])
                                S.op("DVE", lambda v, b=bnq, h=h, nch=nch, qkd=qkd:
                                     v.tensor_tensor(
                                         qkd[:, nch, :], b[:, 0:BLK],
                                         decayT[:, h, nch * BLK:(nch + 1) * BLK],
                                         ALU.mult),
                                     reads=[rbq, r_decayT[h]], writes=[rqkd])
                            bna, rba = nbank()
                            for nch in range(2):
                                S.op("PE", lambda t, b=bna, nch=nch, m0=m0,
                                     vtm3=vtm3, qkd=qkd: t.matmul(
                                         b[:, 0:BLK],
                                         vtm3[:, m0 + nch * 128:m0 + (nch + 1) * 128],
                                         qkd[:, nch, :],
                                         start=(nch == 0), stop=(nch == 1)),
                                     reads=[rvtm3, rqkd], writes=[rba])
                            S.op("DVE", lambda v, b=bni, h=h, t1=t1: v.tensor_tensor(
                                t1[:], b[:, 0:BLK], qd_bc[:, h, :], ALU.mult),
                                reads=[rbi, r_qdbc[h]], writes=[rt1])
                            S.op("DVE", lambda v, b=bna, h=h, m0=m0, t1=t1:
                                 v.tensor_tensor(
                                     o_fm[:, h, m0:m0 + BLK], t1[:], b[:, 0:BLK],
                                     ALU.add),
                                 reads=[rt1, rba], writes=[r_ofm[h]])
                            bnA, rbA = nbank()
                            for nch in range(2):
                                S.op("PE", lambda t, b=bnA, nch=nch, m0=m0,
                                     ktm3=ktm3, vtm3=vtm3: t.matmul(
                                         b[:, 0:HD],
                                         ktm3[:, m0 + nch * 128:m0 + (nch + 1) * 128],
                                         vtm3[:, m0 + nch * 128:m0 + (nch + 1) * 128],
                                         start=(nch == 0), stop=(nch == 1)),
                                     reads=[rktm3, rvtm3], writes=[rbA])
                            S.op("DVE", lambda v, b=bnA, h=h, cur=cur, nxt=nxt:
                                 v.scalar_tensor_tensor(
                                     kv[nxt][:, h, :], kv[cur][:, h, :],
                                     bd_bc[:, h:h + 1], b[:, 0:HD],
                                     ALU.mult, ALU.add),
                                 reads=[r_kv[cur][h], r_bdbc, rbA],
                                 writes=[r_kv[nxt][h]])
                            S.op("ACT", lambda s, h=h, nxt=nxt: s.copy(
                                kv_bf[nxt][:, h, :], kv[nxt][:, h, :]),
                                reads=[r_kv[nxt][h]], writes=[r_kvbf[nxt][h]])

                    def var_step(h):
                        ob = osq[h % 2]
                        rob = r_osq[h % 2]
                        S.op("ACT", lambda s, ob=ob, h=h: s.activation(
                            ob[:], o_fm[:, h, :], AF.Square),
                            reads=[r_ofm[h]], writes=[rob])
                        S.op("PE", lambda t, ob=ob, h=h: t.matmul(
                            pb[5][0:1, 0:TT], ones_colb[:], ob[:],
                            start=(h == 0), stop=(h == NH - 1)),
                            reads=[rob, r_ones], writes=[pb_res[5]])

                    for h in range(NH + 1):
                        if h < NH:
                            at_proj(h)
                        if h >= 1:
                            at_tail(h - 1)
                            var_step(h - 1)

                def sec_rms():
                    S.op("ACT", lambda s: s.activation(
                        var_row[0:1, :], pb[5][0:1, 0:TT], AF.Sqrt,
                        scale=cst[0:1, 1:2], bias=cst[0:1, 2:3]),
                        reads=[pb_res[5]], writes=[r_var])
                    S.op("DVE", lambda v: v.reciprocal(var_row[0:1, :], var_row[0:1, :]),
                         reads=[r_var], writes=[r_var])
                    bnr, rbr = nbank()
                    S.op("PE", lambda t, b=bnr: t.matmul(
                        b[:, 0:TT], ones_rowf[:], var_row[0:1, :],
                        start=True, stop=True),
                        reads=[r_ones, r_var], writes=[rbr])
                    S.op("DVE", lambda v, b=bnr: v.tensor_copy(rstd_bc[:], b[:, 0:TT]),
                         reads=[rbr], writes=[r_rstd])

                def sec_gate():
                    for g2 in range(8):
                        kind, meta, off, w = ttab[tib[0]]; tib[0] += 1
                        assert kind == "wg" and meta == g2
                        sl, rsl = next_slot(kind, meta, off, w)
                        slv = sl[:, 0:w].rearrange("p (c w) -> p c w", c=NFC)
                        for o2 in range(2):
                            oc = g2 * 2 + o2
                            bnk, rbk = nbank()
                            for fc in range(NFC):
                                S.op("PE", lambda t, b=bnk, fc=fc, o2=o2, slv=slv,
                                     xn_sl=xn_sl:
                                     t.matmul(b[:], slv[:, fc, o2 * 128:(o2 + 1) * 128],
                                              xn_sl[:, fc, :],
                                              start=(fc == 0), stop=(fc == NFC - 1)),
                                     reads=[rsl, r_xnsl_t], writes=[rbk])
                            S.op("ACT", lambda s, b=bnk: s.activation(
                                g_tmp[:], b[:], AF.Sigmoid),
                                reads=[rbk], writes=[r_gtmp])
                            S.op("DVE", lambda v, oc=oc: v.tensor_tensor(
                                t2[:], g_tmp[:], o_fm[:, oc, :], ALU.mult),
                                reads=[r_gtmp, r_ofm[oc]], writes=[r_t2])
                            S.op("DVE", lambda v, oc=oc: v.scalar_tensor_tensor(
                                o_fm[:, oc, :], t2[:], norm_sb[:, oc:oc + 1],
                                rstd_bc[:], ALU.mult, ALU.mult),
                                reads=[r_t2, r_norm, r_rstd], writes=[r_ofm[oc]])

                def sec_attny():
                    for fq in range(8):
                        kind, meta, off, w = ttab[tib[0]]; tib[0] += 1
                        assert kind == "wo" and meta == fq
                        sl, rsl = next_slot(kind, meta, off, w)
                        slv = sl[:, 0:w].rearrange("p (c w) -> p c w", c=NFC)
                        for mc in range(4):
                            bnk, rbk = nbank()
                            for oc in range(NFC):
                                S.op("PE", lambda t, b=bnk, oc=oc, mc=mc, slv=slv:
                                     t.matmul(b[:, 0:256],
                                              o_fm[:, oc, mc * 128:(mc + 1) * 128],
                                              slv[:, oc, :],
                                              start=(oc == 0), stop=(oc == NFC - 1)),
                                     reads=[r_ofm[oc], rsl], writes=[rbk])
                            if attn_first:
                                S.op("DVE", lambda v, b=bnk, mc=mc, fq=fq:
                                     v.tensor_copy(
                                         acc[:, mc, fq * 256:(fq + 1) * 256],
                                         b[:, 0:256]),
                                     reads=[rbk], writes=[r_acc[mc]])
                            else:
                                S.op("DVE", lambda v, b=bnk, mc=mc, fq=fq:
                                     v.tensor_tensor(
                                         acc[:, mc, fq * 256:(fq + 1) * 256],
                                         acc[:, mc, fq * 256:(fq + 1) * 256],
                                         b[:, 0:256], ALU.add),
                                     reads=[rbk, r_acc[mc]], writes=[r_acc[mc]])


                sec_logits()
                if attn_first:
                    sec_attn()
                    sec_rms()
                    sec_gate()
                    sec_attny()
                    sec_experts()
                else:
                    sec_experts()
                    sec_attn()
                    sec_rms()
                    sec_gate()
                    sec_attny()

                ensure_chunks(min(consumed[0] + NSLOT - 1, len(chunks_flat) - 1))
                emit_xout(t0)

            S.op("SP", None, reads=[r_out])

        S.finalize()
        with ExitStack() as semctx:
            semh = {}
            for e in _ENGS:
                semh["E:" + e] = semctx.enter_context(nc.semaphore("sem_" + e))
            for name in S.dma_sem_names:
                semh[name] = semctx.enter_context(
                    nc.semaphore(name.replace(":", "_")))
            with nc.Block() as block:
                S.emit(block, semh)

    return nc


_NC_CACHE = {}


def kernel(**inputs):
    in_maps, table = _prep(inputs)
    key = "main"
    if key not in _NC_CACHE:
        _NC_CACHE[key] = _build_nc(table)
    nc = _NC_CACHE[key]
    res = run_bass_kernel_spmd(nc, in_maps, core_ids=list(range(NCORE)))
    out = np.concatenate([res.results[c]["out"] for c in range(NCORE)], axis=0)
    return out.reshape(B, N, Hd)


# version 17
# speedup vs baseline: 1.0483x; 1.0483x over previous
import numpy as np
from contextlib import ExitStack

import concourse.bass as bass
import concourse.mybir as mybir
from concourse.bass_utils import run_bass_kernel_spmd

F32 = mybir.dt.float32
BF16 = mybir.dt.bfloat16
AF = mybir.ActivationFunctionType
ALU = mybir.AluOpType
AX = mybir.AxisListType

B, N, Hd = 2, 8192, 2048
NH, HD = 16, 128
BLK = 256
NCORE = 8
TPC = B * N // NCORE
NBLK = TPC // BLK
NTILE = 4
TT = TPC // NTILE
NFC = Hd // 128
EXP_I = 1408
NIC = EXP_I // 128
MOE_E = 8
NEXP = 10
SH_I = 2816
EPS_ATTN = 1e-6
EPS_LN = 1e-5
BIGIDX = 30000.0
BIGEXPN = 1.0e7
NSLOT = 3
SLOTW = 6144

_ENGS = ("PE", "ACT", "DVE", "POOL", "SP")


class Res:
    __slots__ = ("name", "w", "rs", "sem", "cnt")

    def __init__(self, name, sem=None):
        self.name = name
        self.w = None
        self.rs = []
        self.sem = sem
        self.cnt = 0


class Op:
    __slots__ = ("eng", "fn", "deps", "need", "mno", "dma_res", "dma_inc",
                 "dma_val", "waits", "snap", "idx")

    def __init__(self, eng, fn):
        self.eng = eng
        self.fn = fn
        self.idx = 0
        self.deps = []
        self.need = False
        self.mno = 0
        self.dma_res = None
        self.dma_inc = 16
        self.dma_val = 0
        self.waits = []
        self.snap = None

    def key_val(self):
        if self.dma_res is not None:
            return (self.dma_res.sem, self.dma_val)
        return ("E:" + self.eng, self.mno)


class Sched:
    def __init__(self):
        self.order = []
        self.per_eng = {e: [] for e in _ENGS}
        self.dma_sem_names = []

    def res(self, name):
        return Res(name)

    def dres(self, name):
        sem = "S:" + name
        self.dma_sem_names.append(sem)
        r = Res(name, sem=sem)
        if not hasattr(self, "dress"):
            self.dress = []
        self.dress.append(r)
        return r

    def op(self, eng, fn, reads=(), writes=(), dma_into=None, dma_inc=16):
        o = Op(eng, fn)
        best = {}
        dmas = []

        def dep(d):
            if d is None or d is o:
                return
            if d.dma_res is not None:
                if d not in dmas:
                    dmas.append(d)
                return
            if d.eng == eng and eng == "PE":
                return
            b = best.get(d.eng)
            if b is None or b.idx < d.idx:
                best[d.eng] = d

        for r in reads:
            dep(r.w)
        for r in writes:
            dep(r.w)
            for d in r.rs:
                dep(d)
        if dma_into is not None:
            dep(dma_into.w)
            for d in dma_into.rs:
                dep(d)
        o.deps = dmas + list(best.values())
        for r in reads:
            r.rs.append(o)
        for r in writes:
            r.w = o
            r.rs = []
        if dma_into is not None:
            o.dma_res = dma_into
            o.dma_inc = dma_inc
            dma_into.cnt += 1
            o.dma_val = dma_inc * dma_into.cnt
            dma_into.w = o
            dma_into.rs = []
        o.idx = len(self.per_eng[eng])
        self.order.append(o)
        self.per_eng[eng].append(o)
        return o

    def barrier(self):
        lasts = []
        for e in _ENGS:
            for d in reversed(self.per_eng[e]):
                if d.fn is not None or d.dma_res is not None:
                    lasts.append(d)
                    break
        dma_lasts = [r.w for r in getattr(self, "dress", []) if r.w is not None]
        for e in _ENGS:
            o = Op(e, None)
            for d in lasts + dma_lasts:
                if (d.eng != e or d.dma_res is not None) and d not in o.deps:
                    o.deps.append(d)
            self.order.append(o)
            self.per_eng[e].append(o)

    def finalize(self):
        for o in self.order:
            for d in o.deps:
                if d.dma_res is None:
                    d.need = True
        for e in _ENGS:
            c = 0
            for o in self.per_eng[e]:
                if o.need:
                    c += 1
                    o.mno = c
        observed = {e: {} for e in _ENGS}
        for o in self.order:
            obs = observed[o.eng]
            for d in o.deps:
                key, val = d.key_val()
                if obs.get(key, 0) >= val:
                    continue
                o.waits.append((key, val))
                if d.snap:
                    for k2, v2 in d.snap.items():
                        if obs.get(k2, 0) < v2:
                            obs[k2] = v2
                if obs.get(key, 0) < val:
                    obs[key] = val
            if o.need or o.dma_res is not None:
                snap = dict(obs)
                key, val = o.key_val()
                snap[key] = val
                o.snap = snap

    def emit(self, block, semh):
        def body_for(engname):
            ops = self.per_eng[engname]

            def body(h):
                for o in ops:
                    for (key, val) in o.waits:
                        h.wait_ge(semh[key], val)
                    if o.fn is None:
                        continue
                    inst = o.fn(h)
                    if o.dma_res is not None:
                        inst.then_inc(semh[o.dma_res.sem], o.dma_inc)
                    elif o.need:
                        inst.then_inc(semh["E:" + engname], 1)
            return body

        block.tensor(body_for("PE"))
        block.scalar(body_for("ACT"))
        block.vector(body_for("DVE"))
        block.gpsimd(body_for("POOL"))
        block.sync(body_for("SP"))


def _chunk_lhsT(w, ncols):
    return np.ascontiguousarray(w.reshape(NFC, 128, ncols).transpose(1, 0, 2))


def _chunk_rhs(w, nrow_chunks, width):
    return np.ascontiguousarray(w.reshape(nrow_chunks, 128, width).transpose(1, 0, 2))


IC_GROUPS = [(0, 2), (2, 4), (4, 6), (6, 8), (8, 10), (10, 11)]


def _build_stream(inp):
    wqkvT = np.ascontiguousarray(inp["L0_attn_w_qkv"].T)
    wgT = np.ascontiguousarray(inp["L0_attn_w_gate"].T)
    woT = np.ascontiguousarray(inp["L0_attn_w_out"].T)
    w1 = inp["L0_ffn_w1"]
    w3 = inp["L0_ffn_w3"]
    w2 = inp["L0_ffn_w2"]
    sw1 = inp["L0_ffn_sw1"]
    sw3 = inp["L0_ffn_sw3"]
    sw2 = inp["L0_ffn_sw2"]

    experts = []
    for e in range(MOE_E):
        experts.append((w1[e], w3[e], w2[e]))
    experts.append((sw1[:, :EXP_I], sw3[:, :EXP_I], sw2[:EXP_I]))
    experts.append((sw1[:, EXP_I:], sw3[:, EXP_I:], sw2[EXP_I:]))

    parts = []
    table = []
    off = [0]

    def add(kind, meta, arr2d):
        w = arr2d.shape[-1] * (arr2d.shape[1] if arr2d.ndim == 3 else 1)
        flat = arr2d.reshape(128, -1)
        assert flat.shape[1] == w and w <= SLOTW, (kind, w)
        parts.append(flat)
        table.append((kind, meta, off[0], w))
        off[0] += w

    for e, (we1, we3, we2) in enumerate(experts):
        for gi, (i0, i1) in enumerate(IC_GROUPS):
            add("w1", (e, gi), _chunk_lhsT(we1[:, i0 * 128:i1 * 128], (i1 - i0) * 128))
            add("w3", (e, gi), _chunk_lhsT(we3[:, i0 * 128:i1 * 128], (i1 - i0) * 128))
        for fb in range(4):
            add("w2", (e, fb), _chunk_rhs(we2[:, fb * 512:(fb + 1) * 512], NIC, 512))
    for h in range(NH):
        add("wq", h, _chunk_lhsT(wqkvT[:, h * 384:h * 384 + 128], 128))
    for g in range(8):
        add("wg", g, _chunk_lhsT(wgT[:, g * 256:(g + 1) * 256], 256))
    for fq in range(8):
        add("wo", fq, _chunk_rhs(woT[:, fq * 256:(fq + 1) * 256], 16, 256))

    stream = np.ascontiguousarray(np.concatenate(parts, axis=1), dtype=np.float32)
    return stream, table


def _prep(inputs):
    x = np.asarray(inputs["L0_attn_hidden_states"], dtype=np.float32)
    xf = np.ascontiguousarray(x.reshape(B * N, Hd))

    stream, table = _build_stream(inputs)

    wqkvT = np.ascontiguousarray(inputs["L0_attn_w_qkv"].T)
    wkv = np.empty((128, NH, NFC, 256), np.float32)
    for h in range(NH):
        cols = wqkvT[:, h * 384 + 128: h * 384 + 384]
        wkv[:, h] = cols.reshape(NFC, 128, 256).transpose(1, 0, 2)
    wkv = np.ascontiguousarray(wkv)

    gwT = np.ascontiguousarray(
        inputs["L0_ffn_gate_w"].T.reshape(NFC, 128, MOE_E).transpose(1, 0, 2))
    norm_col = np.ascontiguousarray(
        np.asarray(inputs["L0_attn_norm_w"], np.float32).reshape(NFC, 128).T)
    slopes = np.asarray(inputs["L0_attn_slope_rate"], np.float32).reshape(NH)
    slopes_row = np.ascontiguousarray(slopes.reshape(1, NH))

    arr_row = np.arange(1, BLK + 1, dtype=np.float32).reshape(1, BLK)
    arrK = np.empty((128, 2), np.float32)
    for j in range(2):
        for p in range(128):
            arrK[p, j] = float(BLK - (j * 128 + p + 1))
    idxT = np.empty((128, 2, BLK), np.float32)
    for j in range(2):
        for p in range(128):
            n_ = j * 128 + p
            for m in range(BLK):
                idxT[p, j, m] = float(m - n_) if m >= n_ else BIGIDX
    ident = np.eye(128, dtype=np.float32)

    shared = dict(wstream=stream, wkv=wkv, gwT=gwT, norm_col=norm_col,
                  slopes_row=slopes_row, arr_row=arr_row, arrK=arrK,
                  idxT=np.ascontiguousarray(idxT.reshape(128, 2 * BLK)),
                  ident=ident)

    in_maps = []
    for c in range(NCORE):
        cg = c % 4
        expn = np.full((1, NCORE), BIGEXPN, np.float32)
        for r in range(NCORE):
            rg = r % 4
            if (r // 4) == (c // 4) and rg < cg:
                expn[0, r] = float(BLK * NBLK * (cg - rg - 1))
        im = dict(shared)
        im["x_in"] = np.ascontiguousarray(xf[c * TPC:(c + 1) * TPC])
        im["expn"] = expn
        in_maps.append(im)
    return in_maps, table


def _build_nc(table):
    nc = bass.Bass("TRN2", target_bir_lowering=False)

    x_in = nc.dram_tensor("x_in", [TPC, Hd], F32, kind="ExternalInput")
    wstream = nc.dram_tensor("wstream", [128, table[-1][2] + table[-1][3]], F32,
                             kind="ExternalInput")
    wkv_d = nc.dram_tensor("wkv", [128, NH, NFC, 256], F32, kind="ExternalInput")
    gw_d = nc.dram_tensor("gwT", [128, NFC, MOE_E], F32, kind="ExternalInput")
    norm_d = nc.dram_tensor("norm_col", [128, NFC], F32, kind="ExternalInput")
    slopes_d = nc.dram_tensor("slopes_row", [1, NH], F32, kind="ExternalInput")
    arr_d = nc.dram_tensor("arr_row", [1, BLK], F32, kind="ExternalInput")
    arrK_d = nc.dram_tensor("arrK", [128, 2], F32, kind="ExternalInput")
    idxT_d = nc.dram_tensor("idxT", [128, 2 * BLK], F32, kind="ExternalInput")
    ident_d = nc.dram_tensor("ident", [128, 128], F32, kind="ExternalInput")
    expn_d = nc.dram_tensor("expn", [1, NCORE], F32, kind="ExternalInput")
    out_d = nc.dram_tensor("out", [TPC, Hd], F32, kind="ExternalOutput")

    xn_sc = nc.dram_tensor("xn_sc", [128, NFC, TPC], BF16)
    wstream_bf = nc.dram_tensor("wstream_bf",
                                [128, table[-1][2] + table[-1][3]], BF16)
    kvf_sc = nc.dram_tensor("kvf_sc", [128, 3, NH, TPC], BF16)
    cc_in = nc.dram_tensor("cc_in", [128, NH * HD], F32)
    cc_gather = nc.dram_tensor("cc_gather", [NCORE * 128, NH * HD], F32)

    S = Sched()

    with ExitStack() as top:
        def sb(name, shape, dt):
            return top.enter_context(nc.sbuf_tensor(name, shape, dt))

        def psum(name, shape, dt):
            return top.enter_context(nc.psum_tensor(name, shape, dt))

        pb = [psum(f"pb{i}", [128, 512], F32) for i in range(6)]
        pt2 = [psum(f"pt{i}", [128, 1024], BF16) for i in range(2)]
        pb_res = [S.res(f"pb{i}") for i in range(6)]
        pt_res2 = [S.res(f"pt{i}") for i in range(2)]
        bank_i = [0]

        def nbank():
            i = bank_i[0] % 5
            bank_i[0] += 1
            return pb[i], pb_res[i]

        ident_sb = sb("ident_sb", [128, 128], BF16)
        ident_f32 = sb("ident_f32", [128, 128], F32)
        ones_colb = sb("ones_colb", [128, 1], BF16)
        ones_rowf = sb("ones_rowf", [1, 128], F32)
        cst = sb("cst", [128, 4], F32)
        gw_sb = sb("gw_sb", [128, NFC, MOE_E], BF16)
        norm_sb = sb("norm_sb", [128, NFC], F32)
        qd_bc = sb("qd_bc", [128, NH, BLK], BF16)
        bd_bc = sb("bd_bc", [128, NH], F32)
        kd_all = sb("kd_all", [128, NH, 2], F32)
        decayT = sb("decayT", [128, NH, 2 * BLK], BF16)
        coef_bc = sb("coef_bc", [128, NH * NCORE], F32)
        kv = [sb(f"kv{i}", [128, NH, HD], F32) for i in range(2)]
        kv_bf = [sb(f"kvbf{i}", [128, NH, HD], BF16) for i in range(2)]

        r_ident = S.dres("ident")
        r_identf = S.dres("identf")
        r_ones = S.res("ones")
        r_cst = S.res("cst")
        r_gw = S.dres("gw")
        r_norm = S.dres("norm")
        r_slopes = S.dres("slopes")
        r_negsr = S.res("negsr")
        r_negsb = S.res("negsb")
        r_arr = S.dres("arr")
        r_arrK = S.dres("arrK")
        r_idxT = S.dres("idxT")
        r_expn = S.dres("expn")
        r_qdrow = S.res("qdrow")
        r_qdbc = [S.res(f"qdbc{h}") for h in range(NH)]
        r_bdrow = S.res("bdrow")
        r_bdbc = S.res("bdbc")
        r_kd = [S.res(f"kd{h}") for h in range(NH)]
        r_decayT = [S.res(f"decayT{h}") for h in range(NH)]
        r_coefrow = S.res("coefrow")
        r_coefbc = S.res("coefbc")
        r_kv = [[S.res(f"kv{i}_{h}") for h in range(NH)] for i in range(2)]
        r_kvbf = [[S.res(f"kvbf{i}_{h}") for h in range(NH)] for i in range(2)]
        r_xnsc = [S.dres(f"xnsc{b}") for b in range(NBLK)]
        r_ccin = S.dres("ccin")
        r_ccg = S.dres("ccg")
        r_out = S.dres("out")
        r_kvfsc = S.dres("kvfsc")

        with ExitStack() as pro:
            def sbp(name, shape, dt):
                return pro.enter_context(nc.sbuf_tensor(name, shape, dt))

            slopes_sb = sbp("slopes_sb", [1, NH], F32)
            negs_row = sbp("negs_row", [1, NH], F32)
            negs_bc = sbp("negs_bc", [128, NH], F32)
            arr_sb = sbp("arr_sb", [1, BLK], F32)
            arrK_sb = sbp("arrK_sb", [128, 2], F32)
            idxT_sb = sbp("idxT_sb", [128, 2 * BLK], F32)
            expn_sb = sbp("expn_sb", [1, NCORE], F32)
            qd_row = sbp("qd_row", [1, NH * BLK], F32)
            bd_row = sbp("bd_row", [1, NH], F32)
            coef_row = sbp("coef_row", [1, NH * NCORE], F32)
            dt_tmp = sbp("dt_tmp", [128, 2 * BLK], F32)
            r_dttmp = S.res("dttmp")

            S.op("POOL", lambda g: g.dma_start(out=ident_sb[:], in_=ident_d[:, :]),
                 dma_into=r_ident)
            S.op("SP", lambda s: s.dma_start(out=ident_f32[:], in_=ident_d[:, :]),
                 dma_into=r_identf)
            S.op("POOL", lambda g: g.dma_start(out=gw_sb[:], in_=gw_d[:, :, :]),
                 dma_into=r_gw)
            S.op("SP", lambda s: s.dma_start(out=norm_sb[:], in_=norm_d[:, :]),
                 dma_into=r_norm)
            S.op("SP", lambda s: s.dma_start(out=slopes_sb[:], in_=slopes_d[:, :]),
                 dma_into=r_slopes)
            S.op("SP", lambda s: s.dma_start(out=arr_sb[:], in_=arr_d[:, :]),
                 dma_into=r_arr)
            S.op("SP", lambda s: s.dma_start(out=arrK_sb[:], in_=arrK_d[:, :]),
                 dma_into=r_arrK)
            S.op("SP", lambda s: s.dma_start(out=idxT_sb[:], in_=idxT_d[:, :]),
                 dma_into=r_idxT)
            S.op("SP", lambda s: s.dma_start(out=expn_sb[:], in_=expn_d[:, :]),
                 dma_into=r_expn)
            S.op("DVE", lambda v: v.memset(ones_colb[:], 1.0), writes=[r_ones])
            S.op("DVE", lambda v: v.memset(ones_rowf[:], 1.0), writes=[r_ones])
            S.op("DVE", lambda v: v.memset(cst[:, 0:1], EPS_LN), writes=[r_cst])
            S.op("DVE", lambda v: v.memset(cst[:, 1:2], 1.0 / Hd), writes=[r_cst])
            S.op("DVE", lambda v: v.memset(cst[:, 2:3], EPS_ATTN), writes=[r_cst])
            S.op("DVE", lambda v: v.memset(cst[:, 3:4], 1.0), writes=[r_cst])

            S.op("DVE", lambda v: v.tensor_scalar(negs_row[:], slopes_sb[:], -1.0,
                                                  None, ALU.mult),
                 reads=[r_slopes], writes=[r_negsr])
            bnk, rb_ = nbank()
            S.op("PE", lambda t, b=bnk: t.matmul(b[:, 0:NH], ones_rowf[:],
                                                 negs_row[:],
                                                 start=True, stop=True),
                 reads=[r_ones, r_negsr], writes=[rb_])
            S.op("DVE", lambda v, b=bnk: v.tensor_copy(negs_bc[:], b[:, 0:NH]),
                 reads=[rb_], writes=[r_negsb])

            for h in range(NH):
                S.op("ACT", lambda s, h=h: s.activation(
                    qd_row[0:1, h * BLK:(h + 1) * BLK], arr_sb[:],
                    AF.Exp, scale=negs_row[0:1, h:h + 1]),
                    reads=[r_arr, r_negsr], writes=[r_qdrow])
            for h in range(NH):
                bnk, rb_ = nbank()
                S.op("PE", lambda t, b=bnk, h=h: t.matmul(
                    b[:, 0:BLK], ones_rowf[:], qd_row[0:1, h * BLK:(h + 1) * BLK],
                    start=True, stop=True),
                    reads=[r_ones, r_qdrow], writes=[rb_])
                S.op("DVE", lambda v, b=bnk, h=h: v.tensor_copy(qd_bc[:, h, :],
                                                                b[:, 0:BLK]),
                     reads=[rb_], writes=[r_qdbc[h]])
            S.op("DVE", lambda v: v.tensor_scalar(
                bd_row[:], slopes_sb[:], -float(BLK), None, ALU.mult),
                reads=[r_slopes], writes=[r_bdrow])
            S.op("ACT", lambda s: s.activation(bd_row[:], bd_row[:], AF.Exp),
                 reads=[r_bdrow], writes=[r_bdrow])
            bnk, rb_ = nbank()
            S.op("PE", lambda t, b=bnk: t.matmul(b[:, 0:NH], ones_rowf[:],
                                                 bd_row[:], start=True, stop=True),
                 reads=[r_ones, r_bdrow], writes=[rb_])
            S.op("DVE", lambda v, b=bnk: v.tensor_copy(bd_bc[:], b[:, 0:NH]),
                 reads=[rb_], writes=[r_bdbc])
            for h in range(NH):
                S.op("DVE", lambda v, h=h: v.tensor_scalar(
                    kd_all[:, h, :], arrK_sb[:], negs_bc[:, h:h + 1], None,
                    ALU.mult),
                    reads=[r_arrK, r_negsb], writes=[r_kd[h]])
                S.op("ACT", lambda s, h=h: s.activation(kd_all[:, h, :],
                                                        kd_all[:, h, :], AF.Exp),
                     reads=[r_kd[h]], writes=[r_kd[h]])
            for h in range(NH):
                S.op("DVE", lambda v, h=h: v.tensor_scalar(
                    dt_tmp[:], idxT_sb[:], negs_bc[:, h:h + 1], None, ALU.mult),
                    reads=[r_idxT, r_negsb], writes=[r_dttmp])
                S.op("ACT", lambda s, h=h: s.activation(decayT[:, h, :], dt_tmp[:],
                                                        AF.Exp),
                     reads=[r_dttmp], writes=[r_decayT[h]])
            for h in range(NH):
                S.op("DVE", lambda v, h=h: v.tensor_scalar(
                    coef_row[0:1, h * NCORE:(h + 1) * NCORE], expn_sb[:],
                    negs_row[0:1, h:h + 1], None, ALU.mult),
                    reads=[r_expn, r_negsr], writes=[r_coefrow])
            S.op("ACT", lambda s: s.activation(coef_row[:], coef_row[:], AF.Exp),
                 reads=[r_coefrow], writes=[r_coefrow])
            bnk, rb_ = nbank()
            S.op("PE", lambda t, b=bnk: t.matmul(b[:, 0:NH * NCORE], ones_rowf[:],
                                                 coef_row[:], start=True, stop=True),
                 reads=[r_ones, r_coefrow], writes=[rb_])
            S.op("DVE", lambda v, b=bnk: v.tensor_copy(coef_bc[:],
                                                       b[:, 0:NH * NCORE]),
                 reads=[rb_], writes=[r_coefbc])

            for h in range(NH):
                S.op("DVE", lambda v, h=h: v.memset(kv[0][:, h, :], 0.0),
                     writes=[r_kv[0][h]])

        S.barrier()

        with ExitStack() as p0:
            def sb0(name, shape, dt):
                return p0.enter_context(nc.sbuf_tensor(name, shape, dt))

            x_tc = [sb0(f"x_tc{i}", [128, Hd], F32) for i in range(3)]
            xn_tm = [sb0(f"xn_tm{i}", [128, Hd], F32) for i in range(2)]
            sq_b = xn_tm
            st_b = [sb0(f"st{i}", [128, 6], F32) for i in range(2)]
            xn_all = sb0("xn_all", [128, NFC, TPC], BF16)

            r_xtc = [S.dres(f"xtc{i}") for i in range(3)]
            r_xntm = [S.res(f"xntm{i}") for i in range(2)]
            r_sq = r_xntm
            r_st = [S.res(f"st{i}") for i in range(2)]
            r_xnall = [S.res(f"xnall{b}") for b in range(NBLK)]

            def ln_chain(tc):
                bi = tc % 2
                xt, rxt = x_tc[tc % 3], r_xtc[tc % 3]
                sq, rsq = sq_b[bi], r_sq[bi]
                xm, rxm = xn_tm[bi], r_xntm[bi]
                stt, rst = st_b[bi], r_st[bi]
                S.op("SP", lambda s, xt=xt, tc=tc: s.dma_start(
                    out=xt[:], in_=x_in[tc * 128:(tc + 1) * 128, :]),
                    dma_into=rxt)
                S.op("DVE", lambda v, xt=xt, stt=stt: v.reduce_sum(
                    stt[:, 0:1], xt[:], axis=AX.X),
                    reads=[rxt], writes=[rst])
                S.op("DVE", lambda v, stt=stt: v.tensor_scalar(
                    stt[:, 1:2], stt[:, 0:1], -1.0 / Hd, None, ALU.mult),
                    reads=[rst], writes=[rst])
                S.op("ACT", lambda s, xt=xt, sq=sq, stt=stt: s.activation(
                    sq[:], xt[:], AF.Square, bias=stt[:, 1:2]),
                    reads=[rxt, rst], writes=[rsq])
                S.op("DVE", lambda v, sq=sq, stt=stt: v.reduce_sum(
                    stt[:, 2:3], sq[:], axis=AX.X),
                    reads=[rsq], writes=[rst])
                S.op("ACT", lambda s, stt=stt: s.activation(
                    stt[:, 3:4], stt[:, 2:3], AF.Sqrt,
                    scale=cst[:, 1:2], bias=cst[:, 0:1]),
                    reads=[rst, r_cst], writes=[rst])
                S.op("DVE", lambda v, stt=stt: v.reciprocal(
                    stt[:, 4:5], stt[:, 3:4]),
                    reads=[rst], writes=[rst])
                S.op("DVE", lambda v, stt=stt: v.tensor_tensor(
                    stt[:, 5:6], stt[:, 1:2], stt[:, 4:5], ALU.mult),
                    reads=[rst], writes=[rst])
                S.op("DVE", lambda v, xt=xt, xm=xm, stt=stt: v.tensor_scalar(
                    xm[:], xt[:], stt[:, 4:5], stt[:, 5:6],
                    ALU.mult, ALU.add),
                    reads=[rxt, rst], writes=[rxm])

            def ln_stage2(tc):
                bi = tc % 2
                xm, rxm = xn_tm[bi], r_xntm[bi]
                rblk = r_xnall[tc // 2]
                c0, c1 = tc * 128, (tc + 1) * 128
                for q4 in range(4):
                    bnk, rbk = nbank()
                    for j in range(4):
                        fc = q4 * 4 + j
                        S.op("PE", lambda t, xm=xm, fc=fc, j=j, b=bnk: t.transpose(
                            b[:, j * 128:(j + 1) * 128],
                            xm[:, fc * 128:(fc + 1) * 128], ident_f32[:]),
                            reads=[rxm, r_identf], writes=[rbk])
                    for j in range(4):
                        fc = q4 * 4 + j
                        S.op("DVE", lambda v, fc=fc, j=j, b=bnk, c0=c0, c1=c1:
                             v.tensor_copy(
                                 xn_all[:, fc, c0:c1], b[:, j * 128:(j + 1) * 128]),
                             reads=[rbk], writes=[rblk])
                S.op("SP", lambda s, tc=tc, c0=c0, c1=c1: s.dma_start(
                    out=xn_sc[:, :, c0:c1], in_=xn_all[:, :, c0:c1]),
                    reads=[rblk], dma_into=r_xnsc[tc // 2])

            NTC = TPC // 128

            HG = 4
            wkvg = sb0("wkvg", [128, HG, NFC, 256], BF16)
            kb_b = [sb0(f"kb{i}", [128, BLK], BF16) for i in range(3)]
            vb_b = [sb0(f"vb{i}", [128, BLK], BF16) for i in range(3)]
            ktm_b = [sb0(f"ktm{i}", [128, BLK], BF16) for i in range(2)]
            vtm_b = [sb0(f"vtm{i}", [128, BLK], BF16) for i in range(2)]

            r_wkvgh = [S.dres("wkvgA"), S.dres("wkvgB")]
            r_kb = [S.res(f"kb{i}") for i in range(3)]
            r_vb = [S.res(f"vb{i}") for i in range(3)]
            r_ktm = [S.res(f"ktm{i}") for i in range(2)]
            r_vtm = [S.res(f"vtm{i}") for i in range(2)]

            def p1_proj(blk, hh, si):
                rxb = r_xnall[blk]
                rwk = r_wkvgh[hh // 2]
                m0, m1 = blk * BLK, (blk + 1) * BLK
                par3 = si % 3
                kb, rkb = kb_b[par3], r_kb[par3]
                vb, rvb = vb_b[par3], r_vb[par3]
                bnk, rbk = nbank()
                for fc in range(NFC):
                    S.op("PE", lambda t, b=bnk, hh=hh, fc=fc, m0=m0, m1=m1:
                         t.matmul(b[:, 0:BLK], wkvg[:, hh, fc, 0:128],
                                  xn_all[:, fc, m0:m1],
                                  start=(fc == 0), stop=(fc == NFC - 1)),
                         reads=[rwk, rxb], writes=[rbk])
                S.op("ACT", lambda s, b=bnk, kb=kb: s.activation(
                    kb[:], b[:, 0:BLK], AF.Silu),
                    reads=[rbk], writes=[rkb])
                bnk2, rbk2 = nbank()
                for fc in range(NFC):
                    S.op("PE", lambda t, b=bnk2, hh=hh, fc=fc, m0=m0, m1=m1:
                         t.matmul(b[:, 0:BLK], wkvg[:, hh, fc, 128:256],
                                  xn_all[:, fc, m0:m1],
                                  start=(fc == 0), stop=(fc == NFC - 1)),
                         reads=[rwk, rxb], writes=[rbk2])
                S.op("ACT", lambda s, b=bnk2, vb=vb: s.activation(
                    vb[:], b[:, 0:BLK], AF.Silu),
                    reads=[rbk2], writes=[rvb])

            def p1_T(blk, hh, si):
                par3 = si % 3
                par = si % 2
                kb, rkb = kb_b[par3], r_kb[par3]
                vb, rvb = vb_b[par3], r_vb[par3]
                ptb, rptb = pt2[par], pt_res2[par]
                for j in range(2):
                    S.op("PE", lambda t, j=j, kb=kb, ptb=ptb: t.transpose(
                        ptb[:, j * 128:(j + 1) * 128],
                        kb[:, j * 128:(j + 1) * 128], ident_sb[:]),
                        reads=[rkb, r_ident], writes=[rptb])
                    S.op("PE", lambda t, j=j, vb=vb, ptb=ptb: t.transpose(
                        ptb[:, (2 + j) * 128:(3 + j) * 128],
                        vb[:, j * 128:(j + 1) * 128], ident_sb[:]),
                        reads=[rvb, r_ident], writes=[rptb])

            def p1_A(blk, hh, si, hg):
                h = hg * HG + hh
                par3 = si % 3
                par = si % 2
                kb, rkb = kb_b[par3], r_kb[par3]
                ktm, rktm = ktm_b[par], r_ktm[par]
                vtm, rvtm = vtm_b[par], r_vtm[par]
                ptb, rptb = pt2[par], pt_res2[par]
                cur, nxt = blk % 2, (blk + 1) % 2
                for j in range(2):
                    S.op("DVE", lambda v, h=h, j=j, ktm=ktm, ptb=ptb:
                         v.tensor_scalar(
                             ktm[:, j * 128:(j + 1) * 128],
                             ptb[:, j * 128:(j + 1) * 128],
                             kd_all[:, h, j:j + 1], None, ALU.mult),
                         reads=[rptb, r_kd[h]], writes=[rktm])
                    S.op("DVE", lambda v, j=j, vtm=vtm, ptb=ptb:
                         v.tensor_copy(
                             vtm[:, j * 128:(j + 1) * 128],
                             ptb[:, (2 + j) * 128:(3 + j) * 128]),
                         reads=[rptb], writes=[rvtm])
                bnk3, rbk3 = nbank()
                for j in range(2):
                    S.op("PE", lambda t, b=bnk3, j=j, ktm=ktm, vtm=vtm:
                         t.matmul(
                             b[:, 0:HD], ktm[:, j * 128:(j + 1) * 128],
                             vtm[:, j * 128:(j + 1) * 128],
                             start=(j == 0), stop=(j == 1)),
                         reads=[rktm, rvtm], writes=[rbk3])
                S.op("DVE", lambda v, b=bnk3, h=h, cur=cur, nxt=nxt:
                     v.scalar_tensor_tensor(
                         kv[nxt][:, h, :], kv[cur][:, h, :],
                         bd_bc[:, h:h + 1], b[:, 0:HD],
                         ALU.mult, ALU.add),
                     reads=[r_kv[cur][h], r_bdbc, rbk3],
                     writes=[r_kv[nxt][h]])
                for kind, buf, rbuf in ((0, kb, rkb), (1, ktm, rktm),
                                        (2, vtm, rvtm)):
                    S.op("SP", lambda s, kind=kind, buf=buf, h=h, blk=blk:
                         s.dma_start(
                             out=kvf_sc[:, kind, h, blk * BLK:(blk + 1) * BLK],
                             in_=buf[:]),
                         reads=[rbuf], dma_into=r_kvfsc)

            pend = []
            si_box = [0]

            def scan_step(blk, hh, hg):
                si = si_box[0]
                p1_proj(blk, hh, si)
                pend.append((blk, hh, si, hg))
                if len(pend) >= 2:
                    p1_T(pend[-2][0], pend[-2][1], pend[-2][2])
                if len(pend) >= 3:
                    b3, h3, s3, g3 = pend.pop(0)
                    p1_A(b3, h3, s3, g3)
                si_box[0] += 1

            def scan_flush():
                if pend:
                    p1_T(pend[-1][0], pend[-1][1], pend[-1][2])
                while pend:
                    b3, h3, s3, g3 = pend.pop(0)
                    p1_A(b3, h3, s3, g3)

            def wkvg_load(hg, half):
                h0 = hg * HG + half * 2
                S.op("POOL", lambda g, h0=h0, half=half: g.dma_start(
                    out=wkvg[:, half * 2:half * 2 + 2, :, :],
                    in_=wkv_d[:, h0:h0 + 2, :, :]),
                    dma_into=r_wkvgh[half])

            def after_step(blk, hh, hg):
                if blk == NBLK - 1 and hh % 2 == 1 and hg + 1 < NH // HG:
                    wkvg_load(hg + 1, hh // 2)

            wkvg_load(0, 0)
            wkvg_load(0, 1)
            for tc in range(NTC):
                ln_chain(tc)
                if tc >= 1:
                    ln_stage2(tc - 1)
                    if (tc - 1) % 2 == 1:
                        blk = (tc - 1) // 2
                        for hh in range(HG):
                            scan_step(blk, hh, 0)
                            after_step(blk, hh, 0)
            ln_stage2(NTC - 1)
            for hh in range(HG):
                scan_step(NBLK - 1, hh, 0)
                after_step(NBLK - 1, hh, 0)

            for hg in range(1, NH // HG):
                for blk in range(NBLK):
                    for hh in range(HG):
                        scan_step(blk, hh, hg)
                        after_step(blk, hh, hg)
            scan_flush()

        S.barrier()
        slot = [sb(f"slot{i}", [128, SLOTW], BF16) for i in range(NSLOT)]
        r_slot = [S.dres(f"slot{i}") for i in range(NSLOT)]
        r_wsbf = S.dres("wsbf")
        sall = sb("sall", [128, NH * HD], F32)
        r_sall = S.dres("sall")

        def emit_p2a():
            S.op("SP", lambda s: s.dma_start(
                out=cc_in[:, :].rearrange("p (h d) -> p h d", h=NH),
                in_=kv[0][:, :, :]),
                reads=[r_kv[0][h] for h in range(NH)], dma_into=r_ccin)
            S.op("POOL", lambda g: g.collective_compute(
                "AllGather", ALU.bypass, replica_groups=[list(range(NCORE))],
                ins=[cc_in.ap().opt()], outs=[cc_gather.ap().opt()]),
                reads=[r_ccin], dma_into=r_ccg, dma_inc=1)
            for h in range(NH):
                S.op("DVE", lambda v, h=h: v.memset(kv[0][:, h, :], 0.0),
                     writes=[r_kv[0][h]])

        def emit_p2b():
            for r in range(NCORE):
                S.op("SP", lambda s, r=r: s.dma_start(
                    out=sall[:], in_=cc_gather[r * 128:(r + 1) * 128, :]),
                    reads=[r_ccg], dma_into=r_sall)
                cur, nxt = r % 2, (r + 1) % 2
                for h in range(NH):
                    S.op("DVE", lambda v, r=r, h=h, cur=cur, nxt=nxt:
                         v.scalar_tensor_tensor(
                             kv[nxt][:, h, :], sall[:, h * HD:(h + 1) * HD],
                             coef_bc[:, h * NCORE + r: h * NCORE + r + 1],
                             kv[cur][:, h, :], ALU.mult, ALU.add),
                         reads=[r_sall, r_coefbc, r_kv[cur][h]],
                         writes=[r_kv[nxt][h]])
            for h in range(NH):
                S.op("ACT", lambda s, h=h: s.copy(kv_bf[0][:, h, :], kv[0][:, h, :]),
                     reads=[r_kv[0][h]], writes=[r_kvbf[0][h]])

        with ExitStack() as p3:
            def sb3(name, shape, dt):
                return p3.enter_context(nc.sbuf_tensor(name, shape, dt))

            xn_sl2 = [sb3(f"xn_sl{i}", [128, NFC, TT], BF16) for i in range(2)]
            acc = sb3("acc", [128, 4, Hd], F32)
            o_fm = sb3("o_fm", [128, NFC, TT], BF16)
            qf2 = [sb3(f"qf{i}", [128, TT], BF16) for i in range(2)]
            kvf_b = [sb3(f"kvf{i}", [128, 3, TT], BF16) for i in range(2)]
            kf2 = [kvf_b[i][:, 0, :] for i in range(2)]
            ktm32 = [kvf_b[i][:, 1, :] for i in range(2)]
            vtm32 = [kvf_b[i][:, 2, :] for i in range(2)]
            qkd2 = [sb3(f"qkd{i}", [128, 4, BLK], BF16) for i in range(2)]
            t12 = [sb3(f"t1{i}", [128, BLK], F32) for i in range(2)]
            osq = [sb3(f"osq{i}", [128, TT], BF16) for i in range(2)]
            g_tmp = sb3("g_tmp", [128, TT], BF16)
            t2 = sb3("t2", [128, TT], BF16)
            var_row = sb3("var_row", [1, TT], F32)
            rstd_bc = sb3("rstd_bc", [128, TT], BF16)
            ce = sb3("ce", [128, 4, MOE_E], F32)
            lg = sb3("lg", [128, 64], F32)
            tfm = [sb3(f"tfm{i}", [128, TT], BF16) for i in range(NIC)]
            hsb = sb3("hsb", [128, TT], BF16)

            r_xnsl = [S.dres(f"xnsl{i}") for i in range(2)]
            r_acc = [S.dres(f"acc{mc}") for mc in range(4)]
            r_ofm = [S.res(f"ofm{oc}") for oc in range(NFC)]
            r_qf = [S.res(f"qf{i}") for i in range(2)]
            r_kvf = [S.dres(f"kvf{i}") for i in range(2)]
            r_kf = r_kvf
            r_ktm3 = r_kvf
            r_vtm3 = r_kvf
            r_qkd = [S.res(f"qkd{i}") for i in range(2)]
            r_t1 = [S.res(f"t1{i}") for i in range(2)]
            r_osq = [S.res(f"osq{i}") for i in range(2)]
            r_gtmp = S.res("gtmp")
            r_t2 = S.res("t2")
            r_var = S.res("var")
            r_rstd = S.res("rstd")
            r_ce = [S.res(f"ce{mc}") for mc in range(4)]
            r_lg = S.res("lg")
            r_tfm = [S.res(f"tfm{i}") for i in range(NIC)]
            r_hsb = S.res("hsb")

            nxc = NEXP * (len(IC_GROUPS) * 2 + 4)
            table3 = table[nxc:] + table[:nxc]
            chunks_flat = table * (NTILE - 1) + table3
            emitted = [0]
            consumed = [0]
            ntab = len(table)

            def ensure_chunks(upto):
                while emitted[0] <= upto:
                    ci = emitted[0]
                    _, _, off, w = chunks_flat[ci]
                    i = ci % NSLOT
                    if ci < ntab:
                        S.op("POOL", lambda g, i=i, off=off, w=w: g.dma_start(
                            out=slot[i][:, 0:w], in_=wstream[:, off:off + w]),
                            dma_into=r_slot[i])
                        S.op("SP", lambda s, i=i, off=off, w=w: s.dma_start(
                            out=wstream_bf[:, off:off + w], in_=slot[i][:, 0:w]),
                            reads=[r_slot[i]], dma_into=r_wsbf)
                    else:
                        S.op("POOL", lambda g, i=i, off=off, w=w: g.dma_start(
                            out=slot[i][:, 0:w], in_=wstream_bf[:, off:off + w]),
                            reads=[r_wsbf], dma_into=r_slot[i])
                    emitted[0] += 1

            def next_slot(kind, meta, off, w):
                ci = consumed[0]
                ck, cm, co, cw = chunks_flat[ci]
                assert (ck, cm, co, cw) == (kind, meta, off, w), (ci, ck, kind)
                ensure_chunks(ci)
                consumed[0] += 1
                i = ci % NSLOT
                return slot[i], r_slot[i]

            pending_xout = [None]

            def emit_xout(t0x):
                for mc in range(4):
                    S.op("SP", lambda s, t0x=t0x, mc=mc: s.dma_start(
                        out=sall[:],
                        in_=x_in[t0x + mc * 128: t0x + (mc + 1) * 128, :]),
                        dma_into=r_sall)
                    S.op("DVE", lambda v, mc=mc: v.tensor_tensor(
                        acc[:, mc, :], acc[:, mc, :], sall[:], ALU.add),
                        reads=[r_sall, r_acc[mc]], writes=[r_acc[mc]])
                    S.op("SP", lambda s, t0x=t0x, mc=mc: s.dma_start(
                        out=out_d[t0x + mc * 128: t0x + (mc + 1) * 128, :],
                        in_=acc[:, mc, :]),
                        reads=[r_acc[mc]], dma_into=r_out)

            for tile in range(NTILE):
                t0 = tile * TT
                cur0 = 0
                xn_sl = xn_sl2[tile % 2]
                r_xnsl_t = r_xnsl[tile % 2]
                attn_first = (tile == NTILE - 1)
                ttab = table3 if attn_first else table

                def load_xnsl(tl):
                    S.op("SP", lambda s, tl=tl: s.dma_start(
                        out=xn_sl2[tl % 2][:],
                        in_=xn_sc[:, :, tl * TT:(tl + 1) * TT]),
                        reads=[r_xnsc[2 * tl], r_xnsc[2 * tl + 1]],
                        dma_into=r_xnsl[tl % 2])

                if tile == 0:
                    load_xnsl(0)

                tib = [0]

                def sec_logits():
                    for mc in range(4):
                        bnk, rbk = nbank()
                        for fc in range(NFC):
                            S.op("PE", lambda t, b=bnk, fc=fc, mc=mc, xn_sl=xn_sl:
                                 t.matmul(
                                     b[:, 0:MOE_E],
                                     xn_sl[:, fc, mc * 128:(mc + 1) * 128],
                                     gw_sb[:, fc, :],
                                     start=(fc == 0), stop=(fc == NFC - 1)),
                                 reads=[r_xnsl_t, r_gw], writes=[rbk])
                        E8 = MOE_E
                        c0 = mc * E8
                        S.op("DVE", lambda v, b=bnk: v.reduce_max(
                            lg[:, 32:33], b[:, 0:E8], axis=AX.X),
                            reads=[rbk], writes=[r_lg])
                        S.op("DVE", lambda v, b=bnk, c0=c0: v.tensor_scalar(
                            lg[:, c0:c0 + E8], b[:, 0:E8], lg[:, 32:33], None,
                            ALU.subtract),
                            reads=[rbk, r_lg], writes=[r_lg])
                        S.op("DVE", lambda v, c0=c0, mc=mc: v.tensor_scalar(
                            ce[:, mc, :], lg[:, c0:c0 + E8], 0.0, None, ALU.is_ge),
                            reads=[r_lg], writes=[r_ce[mc]])
                        S.op("DVE", lambda v, c0=c0, mc=mc: v.scalar_tensor_tensor(
                            lg[:, 34:34 + E8], ce[:, mc, :], -1.0e9,
                            lg[:, c0:c0 + E8], ALU.mult, ALU.add),
                            reads=[r_ce[mc], r_lg], writes=[r_lg])
                        S.op("DVE", lambda v: v.reduce_max(
                            lg[:, 42:43], lg[:, 34:34 + MOE_E], axis=AX.X),
                            reads=[r_lg], writes=[r_lg])
                        S.op("ACT", lambda s, c0=c0: s.activation(
                            lg[:, 43:43 + MOE_E], lg[:, c0:c0 + MOE_E], AF.Exp),
                            reads=[r_lg], writes=[r_lg])
                        S.op("ACT", lambda s: s.activation(
                            lg[:, 51:52], lg[:, 42:43], AF.Exp),
                            reads=[r_lg], writes=[r_lg])
                        S.op("DVE", lambda v: v.tensor_scalar(
                            lg[:, 52:53], lg[:, 51:52], 1.0, None, ALU.add),
                            reads=[r_lg], writes=[r_lg])
                        S.op("DVE", lambda v: v.reciprocal(
                            lg[:, 53:54], lg[:, 52:53]),
                            reads=[r_lg], writes=[r_lg])
                        S.op("DVE", lambda v, c0=c0: v.tensor_scalar(
                            lg[:, 54:54 + MOE_E], lg[:, c0:c0 + MOE_E],
                            lg[:, 42:43], None, ALU.is_ge),
                            reads=[r_lg], writes=[r_lg])
                        S.op("DVE", lambda v: v.tensor_tensor(
                            lg[:, 43:43 + MOE_E], lg[:, 43:43 + MOE_E],
                            lg[:, 54:54 + MOE_E], ALU.mult),
                            reads=[r_lg], writes=[r_lg])
                        S.op("DVE", lambda v, mc=mc: v.tensor_scalar(
                            ce[:, mc, :], lg[:, 43:43 + MOE_E], lg[:, 53:54],
                            None, ALU.mult),
                            reads=[r_lg], writes=[r_ce[mc]])

                def sec_experts():
                    for e in range(NEXP):
                        if tile == 0 and e == 1:
                            emit_p2a()
                        if tile == 0 and e == 3:
                            emit_p2b()
                        for gi, (i0, i1) in enumerate(IC_GROUPS):
                            kind, meta, off, w = ttab[tib[0]]; tib[0] += 1
                            assert kind == "w1" and meta == (e, gi)
                            sl1, rsl1 = next_slot(kind, meta, off, w)
                            sv1 = sl1[:, 0:w].rearrange("p (c w) -> p c w", c=NFC)
                            kind, meta, off, w = ttab[tib[0]]; tib[0] += 1
                            assert kind == "w3" and meta == (e, gi)
                            sl3, rsl3 = next_slot(kind, meta, off, w)
                            sv3 = sl3[:, 0:w].rearrange("p (c w) -> p c w", c=NFC)
                            for ic in range(i0, i1):
                                jj = ic - i0
                                bnh, rbh = nbank()
                                for fc in range(NFC):
                                    S.op("PE", lambda t, b=bnh, fc=fc, jj=jj, sv1=sv1,
                                         xn_sl=xn_sl:
                                         t.matmul(b[:], sv1[:, fc, jj * 128:(jj + 1) * 128],
                                                  xn_sl[:, fc, :],
                                                  start=(fc == 0), stop=(fc == NFC - 1)),
                                         reads=[rsl1, r_xnsl_t], writes=[rbh])
                                S.op("ACT", lambda s, b=bnh: s.activation(
                                    hsb[:], b[:], AF.Silu),
                                    reads=[rbh], writes=[r_hsb])
                                bng, rbg = nbank()
                                for fc in range(NFC):
                                    S.op("PE", lambda t, b=bng, fc=fc, jj=jj, sv3=sv3,
                                         xn_sl=xn_sl:
                                         t.matmul(b[:], sv3[:, fc, jj * 128:(jj + 1) * 128],
                                                  xn_sl[:, fc, :],
                                                  start=(fc == 0), stop=(fc == NFC - 1)),
                                         reads=[rsl3, r_xnsl_t], writes=[rbg])
                                S.op("DVE", lambda v, b=bng, ic=ic: v.tensor_tensor(
                                    tfm[ic][:], hsb[:], b[:], ALU.mult),
                                    reads=[r_hsb, rbg], writes=[r_tfm[ic]])
                        for fb in range(4):
                            kind, meta, off, w = ttab[tib[0]]; tib[0] += 1
                            assert kind == "w2" and meta == (e, fb)
                            sl2, rsl2 = next_slot(kind, meta, off, w)
                            sv2 = sl2[:, 0:w].rearrange("p (c w) -> p c w", c=NIC)
                            for mc in range(4):
                                bny, rby = nbank()
                                for ic in range(NIC):
                                    S.op("PE", lambda t, b=bny, ic=ic, mc=mc, sv2=sv2:
                                         t.matmul(b[:],
                                                  tfm[ic][:, mc * 128:(mc + 1) * 128],
                                                  sv2[:, ic, :],
                                                  start=(ic == 0), stop=(ic == NIC - 1)),
                                         reads=[r_tfm[ic], rsl2], writes=[rby])
                                if e == 0 and not attn_first:
                                    S.op("DVE", lambda v, b=bny, mc=mc, fb=fb:
                                         v.tensor_scalar(
                                             acc[:, mc, fb * 512:(fb + 1) * 512],
                                             b[:], ce[:, mc, 0:1], None, ALU.mult),
                                         reads=[rby, r_ce[mc]], writes=[r_acc[mc]])
                                elif e < MOE_E:
                                    S.op("DVE", lambda v, b=bny, mc=mc, fb=fb, e=e:
                                         v.scalar_tensor_tensor(
                                             acc[:, mc, fb * 512:(fb + 1) * 512],
                                             b[:], ce[:, mc, e:e + 1],
                                             acc[:, mc, fb * 512:(fb + 1) * 512],
                                             ALU.mult, ALU.add),
                                         reads=[rby, r_ce[mc], r_acc[mc]],
                                         writes=[r_acc[mc]])
                                else:
                                    S.op("DVE", lambda v, b=bny, mc=mc, fb=fb:
                                         v.tensor_tensor(
                                             acc[:, mc, fb * 512:(fb + 1) * 512],
                                             acc[:, mc, fb * 512:(fb + 1) * 512],
                                             b[:], ALU.add),
                                         reads=[rby, r_acc[mc]], writes=[r_acc[mc]])

                def sec_attn():
                    def at_proj(h):
                        kind, meta, off, w = ttab[tib[0]]; tib[0] += 1
                        assert kind == "wq" and meta == h
                        sl, rsl = next_slot(kind, meta, off, w)
                        slv = sl[:, 0:w].rearrange("p (c w) -> p c w", c=NFC)
                        par = h % 2
                        S.op("SP", lambda s, par=par, h=h, t0=t0: s.dma_start(
                            out=kvf_b[par][:, :, :],
                            in_=kvf_sc[:, :, h, t0:t0 + TT]),
                            reads=[r_kvfsc], dma_into=r_kvf[par])
                        dst, rdst = qf2[par], r_qf[par]
                        bnk, rbk = nbank()
                        for fc in range(NFC):
                            S.op("PE", lambda t, b=bnk, fc=fc, slv=slv,
                                 xn_sl=xn_sl:
                                 t.matmul(b[:], slv[:, fc, 0:128],
                                          xn_sl[:, fc, :],
                                          start=(fc == 0), stop=(fc == NFC - 1)),
                                 reads=[rsl, r_xnsl_t], writes=[rbk])
                        S.op("ACT", lambda s, b=bnk, dst=dst: s.activation(
                            dst[:], b[:], AF.Silu),
                            reads=[rbk], writes=[rdst])

                    def at_tail_a(h):
                        par = h % 2
                        qf, kf = qf2[par], kf2[par]
                        ktm3, vtm3 = ktm32[par], vtm32[par]
                        qkd, t1 = qkd2[par], t12[par]
                        rqf, rkf = r_qf[par], r_kf[par]
                        rktm3, rvtm3 = r_ktm3[par], r_vtm3[par]
                        rqkd, rt1 = r_qkd[par], r_t1[par]
                        bni, rbi = nbank()
                        S.op("PE", lambda t, b=bni, h=h, qf=qf:
                             t.matmul(
                                 b[:, 0:BLK], kv_bf[0][:, h, :],
                                 qf[:, 0:BLK], start=True, stop=True),
                             reads=[r_kvbf[0][h], rqf], writes=[rbi])
                        for b_ in range(2):
                            m0 = b_ * BLK
                            for nch in range(2):
                                bnq, rbq = nbank()
                                S.op("PE", lambda t, b=bnq, nch=nch, m0=m0, kf=kf,
                                     qf=qf: t.matmul(
                                         b[:, 0:BLK],
                                         kf[:, m0 + nch * 128:m0 + (nch + 1) * 128],
                                         qf[:, m0:m0 + BLK], start=True, stop=True),
                                     reads=[rkf, rqf], writes=[rbq])
                                S.op("DVE", lambda v, b=bnq, h=h, nch=nch, b_=b_,
                                     qkd=qkd: v.tensor_tensor(
                                         qkd[:, b_ * 2 + nch, :], b[:, 0:BLK],
                                         decayT[:, h, nch * BLK:(nch + 1) * BLK],
                                         ALU.mult),
                                     reads=[rbq, r_decayT[h]], writes=[rqkd])
                        S.op("DVE", lambda v, b=bni, h=h, t1=t1: v.tensor_tensor(
                            t1[:], b[:, 0:BLK], qd_bc[:, h, :], ALU.mult),
                            reads=[rbi, r_qdbc[h]], writes=[rt1])
                        bna, rba = nbank()
                        for nch in range(2):
                            S.op("PE", lambda t, b=bna, nch=nch, vtm3=vtm3,
                                 qkd=qkd: t.matmul(
                                     b[:, 0:BLK],
                                     vtm3[:, nch * 128:(nch + 1) * 128],
                                     qkd[:, nch, :],
                                     start=(nch == 0), stop=(nch == 1)),
                                 reads=[rvtm3, rqkd], writes=[rba])
                        S.op("DVE", lambda v, b=bna, h=h, t1=t1:
                             v.tensor_tensor(
                                 o_fm[:, h, 0:BLK], t1[:], b[:, 0:BLK],
                                 ALU.add),
                             reads=[rt1, rba], writes=[r_ofm[h]])
                        for b_ in range(2):
                            m0 = b_ * BLK
                            cur, nxt = b_ % 2, 1 - b_ % 2
                            bnA, rbA = nbank()
                            for nch in range(2):
                                S.op("PE", lambda t, b=bnA, nch=nch, m0=m0,
                                     ktm3=ktm3, vtm3=vtm3: t.matmul(
                                         b[:, 0:HD],
                                         ktm3[:, m0 + nch * 128:m0 + (nch + 1) * 128],
                                         vtm3[:, m0 + nch * 128:m0 + (nch + 1) * 128],
                                         start=(nch == 0), stop=(nch == 1)),
                                     reads=[rktm3, rvtm3], writes=[rbA])
                            S.op("DVE", lambda v, b=bnA, h=h, cur=cur, nxt=nxt:
                                 v.scalar_tensor_tensor(
                                     kv[nxt][:, h, :], kv[cur][:, h, :],
                                     bd_bc[:, h:h + 1], b[:, 0:HD],
                                     ALU.mult, ALU.add),
                                 reads=[r_kv[cur][h], r_bdbc, rbA],
                                 writes=[r_kv[nxt][h]])
                            S.op("ACT", lambda s, h=h, nxt=nxt: s.copy(
                                kv_bf[nxt][:, h, :], kv[nxt][:, h, :]),
                                reads=[r_kv[nxt][h]], writes=[r_kvbf[nxt][h]])
                        bna1, rba1 = nbank()
                        for nch in range(2):
                            S.op("PE", lambda t, b=bna1, nch=nch, vtm3=vtm3,
                                 qkd=qkd: t.matmul(
                                     b[:, 0:BLK],
                                     vtm3[:, BLK + nch * 128:BLK + (nch + 1) * 128],
                                     qkd[:, 2 + nch, :],
                                     start=(nch == 0), stop=(nch == 1)),
                                 reads=[rvtm3, rqkd], writes=[rba1])
                        return bna1, rba1

                    def at_tail_b(h, bna1, rba1):
                        par = h % 2
                        qf, t1 = qf2[par], t12[par]
                        rqf, rt1 = r_qf[par], r_t1[par]
                        bnj, rbj = nbank()
                        S.op("PE", lambda t, b=bnj, h=h, qf=qf:
                             t.matmul(
                                 b[:, 0:BLK], kv_bf[1][:, h, :],
                                 qf[:, BLK:2 * BLK], start=True, stop=True),
                             reads=[r_kvbf[1][h], rqf], writes=[rbj])
                        S.op("DVE", lambda v, b=bnj, h=h, t1=t1: v.tensor_tensor(
                            t1[:], b[:, 0:BLK], qd_bc[:, h, :], ALU.mult),
                            reads=[rbj, r_qdbc[h]], writes=[rt1])
                        S.op("DVE", lambda v, b=bna1, h=h, t1=t1:
                             v.tensor_tensor(
                                 o_fm[:, h, BLK:2 * BLK], t1[:], b[:, 0:BLK],
                                 ALU.add),
                             reads=[rt1, rba1], writes=[r_ofm[h]])

                    def var_step(h):
                        ob = osq[h % 2]
                        rob = r_osq[h % 2]
                        S.op("ACT", lambda s, ob=ob, h=h: s.activation(
                            ob[:], o_fm[:, h, :], AF.Square),
                            reads=[r_ofm[h]], writes=[rob])
                        S.op("PE", lambda t, ob=ob, h=h: t.matmul(
                            pb[5][0:1, 0:TT], ones_colb[:], ob[:],
                            start=(h == 0), stop=(h == NH - 1)),
                            reads=[rob, r_ones], writes=[pb_res[5]])

                    pa = [None]
                    for h in range(NH + 1):
                        if h >= 1:
                            pa[0] = at_tail_a(h - 1)
                        if h < NH:
                            at_proj(h)
                        if h >= 1:
                            at_tail_b(h - 1, *pa[0])
                            var_step(h - 1)

                def sec_rms():
                    S.op("ACT", lambda s: s.activation(
                        var_row[0:1, :], pb[5][0:1, 0:TT], AF.Sqrt,
                        scale=cst[0:1, 1:2], bias=cst[0:1, 2:3]),
                        reads=[pb_res[5]], writes=[r_var])
                    S.op("DVE", lambda v: v.reciprocal(var_row[0:1, :], var_row[0:1, :]),
                         reads=[r_var], writes=[r_var])
                    bnr, rbr = nbank()
                    S.op("PE", lambda t, b=bnr: t.matmul(
                        b[:, 0:TT], ones_rowf[:], var_row[0:1, :],
                        start=True, stop=True),
                        reads=[r_ones, r_var], writes=[rbr])
                    S.op("DVE", lambda v, b=bnr: v.tensor_copy(rstd_bc[:], b[:, 0:TT]),
                         reads=[rbr], writes=[r_rstd])

                def sec_gate():
                    for g2 in range(8):
                        kind, meta, off, w = ttab[tib[0]]; tib[0] += 1
                        assert kind == "wg" and meta == g2
                        sl, rsl = next_slot(kind, meta, off, w)
                        slv = sl[:, 0:w].rearrange("p (c w) -> p c w", c=NFC)
                        for o2 in range(2):
                            oc = g2 * 2 + o2
                            bnk, rbk = nbank()
                            for fc in range(NFC):
                                S.op("PE", lambda t, b=bnk, fc=fc, o2=o2, slv=slv,
                                     xn_sl=xn_sl:
                                     t.matmul(b[:], slv[:, fc, o2 * 128:(o2 + 1) * 128],
                                              xn_sl[:, fc, :],
                                              start=(fc == 0), stop=(fc == NFC - 1)),
                                     reads=[rsl, r_xnsl_t], writes=[rbk])
                            S.op("ACT", lambda s, b=bnk: s.activation(
                                g_tmp[:], b[:], AF.Sigmoid),
                                reads=[rbk], writes=[r_gtmp])
                            S.op("DVE", lambda v, oc=oc: v.tensor_tensor(
                                t2[:], g_tmp[:], o_fm[:, oc, :], ALU.mult),
                                reads=[r_gtmp, r_ofm[oc]], writes=[r_t2])
                            S.op("DVE", lambda v, oc=oc: v.scalar_tensor_tensor(
                                o_fm[:, oc, :], t2[:], norm_sb[:, oc:oc + 1],
                                rstd_bc[:], ALU.mult, ALU.mult),
                                reads=[r_t2, r_norm, r_rstd], writes=[r_ofm[oc]])

                def sec_attny():
                    for fq in range(8):
                        kind, meta, off, w = ttab[tib[0]]; tib[0] += 1
                        assert kind == "wo" and meta == fq
                        sl, rsl = next_slot(kind, meta, off, w)
                        slv = sl[:, 0:w].rearrange("p (c w) -> p c w", c=NFC)
                        for mc in range(4):
                            bnk, rbk = nbank()
                            for oc in range(NFC):
                                S.op("PE", lambda t, b=bnk, oc=oc, mc=mc, slv=slv:
                                     t.matmul(b[:, 0:256],
                                              o_fm[:, oc, mc * 128:(mc + 1) * 128],
                                              slv[:, oc, :],
                                              start=(oc == 0), stop=(oc == NFC - 1)),
                                     reads=[r_ofm[oc], rsl], writes=[rbk])
                            if attn_first:
                                S.op("DVE", lambda v, b=bnk, mc=mc, fq=fq:
                                     v.tensor_copy(
                                         acc[:, mc, fq * 256:(fq + 1) * 256],
                                         b[:, 0:256]),
                                     reads=[rbk], writes=[r_acc[mc]])
                            else:
                                S.op("DVE", lambda v, b=bnk, mc=mc, fq=fq:
                                     v.tensor_tensor(
                                         acc[:, mc, fq * 256:(fq + 1) * 256],
                                         acc[:, mc, fq * 256:(fq + 1) * 256],
                                         b[:, 0:256], ALU.add),
                                     reads=[rbk, r_acc[mc]], writes=[r_acc[mc]])


                sec_logits()
                if attn_first:
                    sec_attn()
                    sec_rms()
                    sec_gate()
                    sec_attny()
                    sec_experts()
                else:
                    sec_experts()
                    sec_attn()
                    if tile + 1 < NTILE:
                        load_xnsl(tile + 1)
                    sec_rms()
                    sec_gate()
                    sec_attny()

                ensure_chunks(min(consumed[0] + NSLOT - 1, len(chunks_flat) - 1))
                emit_xout(t0)

            S.op("SP", None, reads=[r_out])

        S.finalize()
        with ExitStack() as semctx:
            semh = {}
            for e in _ENGS:
                semh["E:" + e] = semctx.enter_context(nc.semaphore("sem_" + e))
            for name in S.dma_sem_names:
                semh[name] = semctx.enter_context(
                    nc.semaphore(name.replace(":", "_")))
            with nc.Block() as block:
                S.emit(block, semh)

    return nc


_NC_CACHE = {}


def kernel(**inputs):
    in_maps, table = _prep(inputs)
    key = "main"
    if key not in _NC_CACHE:
        _NC_CACHE[key] = _build_nc(table)
    nc = _NC_CACHE[key]
    res = run_bass_kernel_spmd(nc, in_maps, core_ids=list(range(NCORE)))
    out = np.concatenate([res.results[c]["out"] for c in range(NCORE)], axis=0)
    return out.reshape(B, N, Hd)


# version 18 (final)
# speedup vs baseline: 1.0489x; 1.0005x over previous
import numpy as np
from contextlib import ExitStack

import concourse.bass as bass
import concourse.mybir as mybir
from concourse.bass_utils import run_bass_kernel_spmd

F32 = mybir.dt.float32
BF16 = mybir.dt.bfloat16
AF = mybir.ActivationFunctionType
ALU = mybir.AluOpType
AX = mybir.AxisListType

B, N, Hd = 2, 8192, 2048
NH, HD = 16, 128
BLK = 256
NCORE = 8
TPC = B * N // NCORE
NBLK = TPC // BLK
NTILE = 4
TT = TPC // NTILE
NFC = Hd // 128
EXP_I = 1408
NIC = EXP_I // 128
MOE_E = 8
NEXP = 10
SH_I = 2816
EPS_ATTN = 1e-6
EPS_LN = 1e-5
BIGIDX = 30000.0
BIGEXPN = 1.0e7
NSLOT = 3
SLOTW = 6144

_ENGS = ("PE", "ACT", "DVE", "POOL", "SP")


class Res:
    __slots__ = ("name", "w", "rs", "sem", "cnt")

    def __init__(self, name, sem=None):
        self.name = name
        self.w = None
        self.rs = []
        self.sem = sem
        self.cnt = 0


class Op:
    __slots__ = ("eng", "fn", "deps", "need", "mno", "dma_res", "dma_inc",
                 "dma_val", "waits", "snap", "idx")

    def __init__(self, eng, fn):
        self.eng = eng
        self.fn = fn
        self.idx = 0
        self.deps = []
        self.need = False
        self.mno = 0
        self.dma_res = None
        self.dma_inc = 16
        self.dma_val = 0
        self.waits = []
        self.snap = None

    def key_val(self):
        if self.dma_res is not None:
            return (self.dma_res.sem, self.dma_val)
        return ("E:" + self.eng, self.mno)


class Sched:
    def __init__(self):
        self.order = []
        self.per_eng = {e: [] for e in _ENGS}
        self.dma_sem_names = []

    def res(self, name):
        return Res(name)

    def dres(self, name):
        sem = "S:" + name
        self.dma_sem_names.append(sem)
        r = Res(name, sem=sem)
        if not hasattr(self, "dress"):
            self.dress = []
        self.dress.append(r)
        return r

    def op(self, eng, fn, reads=(), writes=(), dma_into=None, dma_inc=16):
        o = Op(eng, fn)
        best = {}
        dmas = []

        def dep(d):
            if d is None or d is o:
                return
            if d.dma_res is not None:
                if d not in dmas:
                    dmas.append(d)
                return
            if d.eng == eng and eng == "PE":
                return
            b = best.get(d.eng)
            if b is None or b.idx < d.idx:
                best[d.eng] = d

        for r in reads:
            dep(r.w)
        for r in writes:
            dep(r.w)
            for d in r.rs:
                dep(d)
        if dma_into is not None:
            dep(dma_into.w)
            for d in dma_into.rs:
                dep(d)
        o.deps = dmas + list(best.values())
        for r in reads:
            r.rs.append(o)
        for r in writes:
            r.w = o
            r.rs = []
        if dma_into is not None:
            o.dma_res = dma_into
            o.dma_inc = dma_inc
            dma_into.cnt += 1
            o.dma_val = dma_inc * dma_into.cnt
            dma_into.w = o
            dma_into.rs = []
        o.idx = len(self.per_eng[eng])
        self.order.append(o)
        self.per_eng[eng].append(o)
        return o

    def barrier(self):
        lasts = []
        for e in _ENGS:
            for d in reversed(self.per_eng[e]):
                if d.fn is not None or d.dma_res is not None:
                    lasts.append(d)
                    break
        dma_lasts = [r.w for r in getattr(self, "dress", []) if r.w is not None]
        for e in _ENGS:
            o = Op(e, None)
            for d in lasts + dma_lasts:
                if (d.eng != e or d.dma_res is not None) and d not in o.deps:
                    o.deps.append(d)
            self.order.append(o)
            self.per_eng[e].append(o)

    def finalize(self):
        for o in self.order:
            for d in o.deps:
                if d.dma_res is None:
                    d.need = True
        for e in _ENGS:
            c = 0
            for o in self.per_eng[e]:
                if o.need:
                    c += 1
                    o.mno = c
        observed = {e: {} for e in _ENGS}
        for o in self.order:
            obs = observed[o.eng]
            for d in o.deps:
                key, val = d.key_val()
                if obs.get(key, 0) >= val:
                    continue
                o.waits.append((key, val))
                if d.snap:
                    for k2, v2 in d.snap.items():
                        if obs.get(k2, 0) < v2:
                            obs[k2] = v2
                if obs.get(key, 0) < val:
                    obs[key] = val
            if o.need or o.dma_res is not None:
                snap = dict(obs)
                key, val = o.key_val()
                snap[key] = val
                o.snap = snap

    def emit(self, block, semh):
        def body_for(engname):
            ops = self.per_eng[engname]

            def body(h):
                for o in ops:
                    for (key, val) in o.waits:
                        h.wait_ge(semh[key], val)
                    if o.fn is None:
                        continue
                    inst = o.fn(h)
                    if o.dma_res is not None:
                        inst.then_inc(semh[o.dma_res.sem], o.dma_inc)
                    elif o.need:
                        inst.then_inc(semh["E:" + engname], 1)
            return body

        block.tensor(body_for("PE"))
        block.scalar(body_for("ACT"))
        block.vector(body_for("DVE"))
        block.gpsimd(body_for("POOL"))
        block.sync(body_for("SP"))


def _chunk_lhsT(w, ncols):
    return np.ascontiguousarray(w.reshape(NFC, 128, ncols).transpose(1, 0, 2))


def _chunk_rhs(w, nrow_chunks, width):
    return np.ascontiguousarray(w.reshape(nrow_chunks, 128, width).transpose(1, 0, 2))


IC_GROUPS = [(0, 2), (2, 4), (4, 6), (6, 8), (8, 10), (10, 11)]


def _build_stream(inp):
    wqkvT = np.ascontiguousarray(inp["L0_attn_w_qkv"].T)
    wgT = np.ascontiguousarray(inp["L0_attn_w_gate"].T)
    woT = np.ascontiguousarray(inp["L0_attn_w_out"].T)
    w1 = inp["L0_ffn_w1"]
    w3 = inp["L0_ffn_w3"]
    w2 = inp["L0_ffn_w2"]
    sw1 = inp["L0_ffn_sw1"]
    sw3 = inp["L0_ffn_sw3"]
    sw2 = inp["L0_ffn_sw2"]

    experts = []
    for e in range(MOE_E):
        experts.append((w1[e], w3[e], w2[e]))
    experts.append((sw1[:, :EXP_I], sw3[:, :EXP_I], sw2[:EXP_I]))
    experts.append((sw1[:, EXP_I:], sw3[:, EXP_I:], sw2[EXP_I:]))

    parts = []
    table = []
    off = [0]

    def add(kind, meta, arr2d):
        w = arr2d.shape[-1] * (arr2d.shape[1] if arr2d.ndim == 3 else 1)
        flat = arr2d.reshape(128, -1)
        assert flat.shape[1] == w and w <= SLOTW, (kind, w)
        parts.append(flat)
        table.append((kind, meta, off[0], w))
        off[0] += w

    for e, (we1, we3, we2) in enumerate(experts):
        for gi, (i0, i1) in enumerate(IC_GROUPS):
            add("w1", (e, gi), _chunk_lhsT(we1[:, i0 * 128:i1 * 128], (i1 - i0) * 128))
            add("w3", (e, gi), _chunk_lhsT(we3[:, i0 * 128:i1 * 128], (i1 - i0) * 128))
        for fb in range(4):
            add("w2", (e, fb), _chunk_rhs(we2[:, fb * 512:(fb + 1) * 512], NIC, 512))
    for h in range(NH):
        add("wq", h, _chunk_lhsT(wqkvT[:, h * 384:h * 384 + 128], 128))
    for g in range(8):
        add("wg", g, _chunk_lhsT(wgT[:, g * 256:(g + 1) * 256], 256))
    for fq in range(8):
        add("wo", fq, _chunk_rhs(woT[:, fq * 256:(fq + 1) * 256], 16, 256))

    stream = np.ascontiguousarray(np.concatenate(parts, axis=1), dtype=np.float32)
    return stream, table


def _prep(inputs):
    x = np.asarray(inputs["L0_attn_hidden_states"], dtype=np.float32)
    xf = np.ascontiguousarray(x.reshape(B * N, Hd))

    stream, table = _build_stream(inputs)

    wqkvT = np.ascontiguousarray(inputs["L0_attn_w_qkv"].T)
    wkv = np.empty((128, NH, NFC, 256), np.float32)
    for h in range(NH):
        cols = wqkvT[:, h * 384 + 128: h * 384 + 384]
        wkv[:, h] = cols.reshape(NFC, 128, 256).transpose(1, 0, 2)
    wkv = np.ascontiguousarray(wkv)

    gwT = np.ascontiguousarray(
        inputs["L0_ffn_gate_w"].T.reshape(NFC, 128, MOE_E).transpose(1, 0, 2))
    norm_col = np.ascontiguousarray(
        np.asarray(inputs["L0_attn_norm_w"], np.float32).reshape(NFC, 128).T)
    slopes = np.asarray(inputs["L0_attn_slope_rate"], np.float32).reshape(NH)
    slopes_row = np.ascontiguousarray(slopes.reshape(1, NH))

    arr_row = np.arange(1, BLK + 1, dtype=np.float32).reshape(1, BLK)
    arrK = np.empty((128, 2), np.float32)
    for j in range(2):
        for p in range(128):
            arrK[p, j] = float(BLK - (j * 128 + p + 1))
    idxT = np.empty((128, 2, BLK), np.float32)
    for j in range(2):
        for p in range(128):
            n_ = j * 128 + p
            for m in range(BLK):
                idxT[p, j, m] = float(m - n_) if m >= n_ else BIGIDX
    ident = np.eye(128, dtype=np.float32)

    shared = dict(wstream=stream, wkv=wkv, gwT=gwT, norm_col=norm_col,
                  slopes_row=slopes_row, arr_row=arr_row, arrK=arrK,
                  idxT=np.ascontiguousarray(idxT.reshape(128, 2 * BLK)),
                  ident=ident)

    in_maps = []
    for c in range(NCORE):
        cg = c % 4
        expn = np.full((1, NCORE), BIGEXPN, np.float32)
        for r in range(NCORE):
            rg = r % 4
            if (r // 4) == (c // 4) and rg < cg:
                expn[0, r] = float(BLK * NBLK * (cg - rg - 1))
        im = dict(shared)
        im["x_in"] = np.ascontiguousarray(xf[c * TPC:(c + 1) * TPC])
        im["expn"] = expn
        in_maps.append(im)
    return in_maps, table


def _build_nc(table):
    nc = bass.Bass("TRN2", target_bir_lowering=False)

    x_in = nc.dram_tensor("x_in", [TPC, Hd], F32, kind="ExternalInput")
    wstream = nc.dram_tensor("wstream", [128, table[-1][2] + table[-1][3]], F32,
                             kind="ExternalInput")
    wkv_d = nc.dram_tensor("wkv", [128, NH, NFC, 256], F32, kind="ExternalInput")
    gw_d = nc.dram_tensor("gwT", [128, NFC, MOE_E], F32, kind="ExternalInput")
    norm_d = nc.dram_tensor("norm_col", [128, NFC], F32, kind="ExternalInput")
    slopes_d = nc.dram_tensor("slopes_row", [1, NH], F32, kind="ExternalInput")
    arr_d = nc.dram_tensor("arr_row", [1, BLK], F32, kind="ExternalInput")
    arrK_d = nc.dram_tensor("arrK", [128, 2], F32, kind="ExternalInput")
    idxT_d = nc.dram_tensor("idxT", [128, 2 * BLK], F32, kind="ExternalInput")
    ident_d = nc.dram_tensor("ident", [128, 128], F32, kind="ExternalInput")
    expn_d = nc.dram_tensor("expn", [1, NCORE], F32, kind="ExternalInput")
    out_d = nc.dram_tensor("out", [TPC, Hd], F32, kind="ExternalOutput")

    xn_sc = nc.dram_tensor("xn_sc", [128, NFC, TPC], BF16)
    wstream_bf = nc.dram_tensor("wstream_bf",
                                [128, table[-1][2] + table[-1][3]], BF16)
    kvf_sc = nc.dram_tensor("kvf_sc", [128, 3, NH, TPC], BF16)
    cc_in = nc.dram_tensor("cc_in", [128, NH * HD], F32)
    cc_gather = nc.dram_tensor("cc_gather", [NCORE * 128, NH * HD], F32)

    S = Sched()

    with ExitStack() as top:
        def sb(name, shape, dt):
            return top.enter_context(nc.sbuf_tensor(name, shape, dt))

        def psum(name, shape, dt):
            return top.enter_context(nc.psum_tensor(name, shape, dt))

        pb = [psum(f"pb{i}", [128, 512], F32) for i in range(6)]
        pt2 = [psum(f"pt{i}", [128, 1024], BF16) for i in range(2)]
        pb_res = [S.res(f"pb{i}") for i in range(6)]
        pt_res2 = [S.res(f"pt{i}") for i in range(2)]
        bank_i = [0]

        def nbank():
            i = bank_i[0] % 5
            bank_i[0] += 1
            return pb[i], pb_res[i]

        ident_sb = sb("ident_sb", [128, 128], BF16)
        ident_f32 = sb("ident_f32", [128, 128], F32)
        ones_colb = sb("ones_colb", [128, 1], BF16)
        ones_rowf = sb("ones_rowf", [1, 128], F32)
        cst = sb("cst", [128, 4], F32)
        gw_sb = sb("gw_sb", [128, NFC, MOE_E], BF16)
        norm_sb = sb("norm_sb", [128, NFC], F32)
        qd_bc = sb("qd_bc", [128, NH, BLK], BF16)
        bd_bc = sb("bd_bc", [128, NH], F32)
        kd_all = sb("kd_all", [128, NH, 2], F32)
        decayT = sb("decayT", [128, NH, 2 * BLK], BF16)
        coef_bc = sb("coef_bc", [128, NH * NCORE], F32)
        kv = [sb(f"kv{i}", [128, NH, HD], F32) for i in range(2)]
        kv_bf = [sb(f"kvbf{i}", [128, NH, HD], BF16) for i in range(2)]

        r_ident = S.dres("ident")
        r_identf = S.dres("identf")
        r_ones = S.res("ones")
        r_cst = S.res("cst")
        r_gw = S.dres("gw")
        r_norm = S.dres("norm")
        r_slopes = S.dres("slopes")
        r_negsr = S.res("negsr")
        r_negsb = S.res("negsb")
        r_arr = S.dres("arr")
        r_arrK = S.dres("arrK")
        r_idxT = S.dres("idxT")
        r_expn = S.dres("expn")
        r_qdrow = S.res("qdrow")
        r_qdbc = [S.res(f"qdbc{h}") for h in range(NH)]
        r_bdrow = S.res("bdrow")
        r_bdbc = S.res("bdbc")
        r_kd = [S.res(f"kd{h}") for h in range(NH)]
        r_decayT = [S.res(f"decayT{h}") for h in range(NH)]
        r_coefrow = S.res("coefrow")
        r_coefbc = S.res("coefbc")
        r_kv = [[S.res(f"kv{i}_{h}") for h in range(NH)] for i in range(2)]
        r_kvbf = [[S.res(f"kvbf{i}_{h}") for h in range(NH)] for i in range(2)]
        r_xnsc = [S.dres(f"xnsc{b}") for b in range(NBLK)]
        r_ccin = S.dres("ccin")
        r_ccg = S.dres("ccg")
        r_out = S.dres("out")
        r_out2 = S.dres("out2")
        r_kvfsc = S.dres("kvfsc")

        with ExitStack() as pro:
            def sbp(name, shape, dt):
                return pro.enter_context(nc.sbuf_tensor(name, shape, dt))

            slopes_sb = sbp("slopes_sb", [1, NH], F32)
            negs_row = sbp("negs_row", [1, NH], F32)
            negs_bc = sbp("negs_bc", [128, NH], F32)
            arr_sb = sbp("arr_sb", [1, BLK], F32)
            arrK_sb = sbp("arrK_sb", [128, 2], F32)
            idxT_sb = sbp("idxT_sb", [128, 2 * BLK], F32)
            expn_sb = sbp("expn_sb", [1, NCORE], F32)
            qd_row = sbp("qd_row", [1, NH * BLK], F32)
            bd_row = sbp("bd_row", [1, NH], F32)
            coef_row = sbp("coef_row", [1, NH * NCORE], F32)
            dt_tmp = sbp("dt_tmp", [128, 2 * BLK], F32)
            r_dttmp = S.res("dttmp")

            S.op("POOL", lambda g: g.dma_start(out=ident_sb[:], in_=ident_d[:, :]),
                 dma_into=r_ident)
            S.op("SP", lambda s: s.dma_start(out=ident_f32[:], in_=ident_d[:, :]),
                 dma_into=r_identf)
            S.op("POOL", lambda g: g.dma_start(out=gw_sb[:], in_=gw_d[:, :, :]),
                 dma_into=r_gw)
            S.op("SP", lambda s: s.dma_start(out=norm_sb[:], in_=norm_d[:, :]),
                 dma_into=r_norm)
            S.op("SP", lambda s: s.dma_start(out=slopes_sb[:], in_=slopes_d[:, :]),
                 dma_into=r_slopes)
            S.op("SP", lambda s: s.dma_start(out=arr_sb[:], in_=arr_d[:, :]),
                 dma_into=r_arr)
            S.op("SP", lambda s: s.dma_start(out=arrK_sb[:], in_=arrK_d[:, :]),
                 dma_into=r_arrK)
            S.op("SP", lambda s: s.dma_start(out=idxT_sb[:], in_=idxT_d[:, :]),
                 dma_into=r_idxT)
            S.op("SP", lambda s: s.dma_start(out=expn_sb[:], in_=expn_d[:, :]),
                 dma_into=r_expn)
            S.op("DVE", lambda v: v.memset(ones_colb[:], 1.0), writes=[r_ones])
            S.op("DVE", lambda v: v.memset(ones_rowf[:], 1.0), writes=[r_ones])
            S.op("DVE", lambda v: v.memset(cst[:, 0:1], EPS_LN), writes=[r_cst])
            S.op("DVE", lambda v: v.memset(cst[:, 1:2], 1.0 / Hd), writes=[r_cst])
            S.op("DVE", lambda v: v.memset(cst[:, 2:3], EPS_ATTN), writes=[r_cst])
            S.op("DVE", lambda v: v.memset(cst[:, 3:4], 1.0), writes=[r_cst])

            S.op("DVE", lambda v: v.tensor_scalar(negs_row[:], slopes_sb[:], -1.0,
                                                  None, ALU.mult),
                 reads=[r_slopes], writes=[r_negsr])
            bnk, rb_ = nbank()
            S.op("PE", lambda t, b=bnk: t.matmul(b[:, 0:NH], ones_rowf[:],
                                                 negs_row[:],
                                                 start=True, stop=True),
                 reads=[r_ones, r_negsr], writes=[rb_])
            S.op("DVE", lambda v, b=bnk: v.tensor_copy(negs_bc[:], b[:, 0:NH]),
                 reads=[rb_], writes=[r_negsb])

            for h in range(NH):
                S.op("ACT", lambda s, h=h: s.activation(
                    qd_row[0:1, h * BLK:(h + 1) * BLK], arr_sb[:],
                    AF.Exp, scale=negs_row[0:1, h:h + 1]),
                    reads=[r_arr, r_negsr], writes=[r_qdrow])
            for h in range(NH):
                bnk, rb_ = nbank()
                S.op("PE", lambda t, b=bnk, h=h: t.matmul(
                    b[:, 0:BLK], ones_rowf[:], qd_row[0:1, h * BLK:(h + 1) * BLK],
                    start=True, stop=True),
                    reads=[r_ones, r_qdrow], writes=[rb_])
                S.op("DVE", lambda v, b=bnk, h=h: v.tensor_copy(qd_bc[:, h, :],
                                                                b[:, 0:BLK]),
                     reads=[rb_], writes=[r_qdbc[h]])
            S.op("DVE", lambda v: v.tensor_scalar(
                bd_row[:], slopes_sb[:], -float(BLK), None, ALU.mult),
                reads=[r_slopes], writes=[r_bdrow])
            S.op("ACT", lambda s: s.activation(bd_row[:], bd_row[:], AF.Exp),
                 reads=[r_bdrow], writes=[r_bdrow])
            bnk, rb_ = nbank()
            S.op("PE", lambda t, b=bnk: t.matmul(b[:, 0:NH], ones_rowf[:],
                                                 bd_row[:], start=True, stop=True),
                 reads=[r_ones, r_bdrow], writes=[rb_])
            S.op("DVE", lambda v, b=bnk: v.tensor_copy(bd_bc[:], b[:, 0:NH]),
                 reads=[rb_], writes=[r_bdbc])
            for h in range(NH):
                S.op("DVE", lambda v, h=h: v.tensor_scalar(
                    kd_all[:, h, :], arrK_sb[:], negs_bc[:, h:h + 1], None,
                    ALU.mult),
                    reads=[r_arrK, r_negsb], writes=[r_kd[h]])
                S.op("ACT", lambda s, h=h: s.activation(kd_all[:, h, :],
                                                        kd_all[:, h, :], AF.Exp),
                     reads=[r_kd[h]], writes=[r_kd[h]])
            for h in range(NH):
                S.op("DVE", lambda v, h=h: v.tensor_scalar(
                    dt_tmp[:], idxT_sb[:], negs_bc[:, h:h + 1], None, ALU.mult),
                    reads=[r_idxT, r_negsb], writes=[r_dttmp])
                S.op("ACT", lambda s, h=h: s.activation(decayT[:, h, :], dt_tmp[:],
                                                        AF.Exp),
                     reads=[r_dttmp], writes=[r_decayT[h]])
            for h in range(NH):
                S.op("DVE", lambda v, h=h: v.tensor_scalar(
                    coef_row[0:1, h * NCORE:(h + 1) * NCORE], expn_sb[:],
                    negs_row[0:1, h:h + 1], None, ALU.mult),
                    reads=[r_expn, r_negsr], writes=[r_coefrow])
            S.op("ACT", lambda s: s.activation(coef_row[:], coef_row[:], AF.Exp),
                 reads=[r_coefrow], writes=[r_coefrow])
            bnk, rb_ = nbank()
            S.op("PE", lambda t, b=bnk: t.matmul(b[:, 0:NH * NCORE], ones_rowf[:],
                                                 coef_row[:], start=True, stop=True),
                 reads=[r_ones, r_coefrow], writes=[rb_])
            S.op("DVE", lambda v, b=bnk: v.tensor_copy(coef_bc[:],
                                                       b[:, 0:NH * NCORE]),
                 reads=[rb_], writes=[r_coefbc])

            for h in range(NH):
                S.op("DVE", lambda v, h=h: v.memset(kv[0][:, h, :], 0.0),
                     writes=[r_kv[0][h]])

        S.barrier()

        with ExitStack() as p0:
            def sb0(name, shape, dt):
                return p0.enter_context(nc.sbuf_tensor(name, shape, dt))

            x_tc = [sb0(f"x_tc{i}", [128, Hd], F32) for i in range(3)]
            xn_tm = [sb0(f"xn_tm{i}", [128, Hd], F32) for i in range(2)]
            sq_b = xn_tm
            st_b = [sb0(f"st{i}", [128, 6], F32) for i in range(2)]
            xn_all = sb0("xn_all", [128, NFC, TPC], BF16)

            r_xtc = [S.dres(f"xtc{i}") for i in range(3)]
            r_xntm = [S.res(f"xntm{i}") for i in range(2)]
            r_sq = r_xntm
            r_st = [S.res(f"st{i}") for i in range(2)]
            r_xnall = [S.res(f"xnall{b}") for b in range(NBLK)]

            def ln_chain(tc):
                bi = tc % 2
                xt, rxt = x_tc[tc % 3], r_xtc[tc % 3]
                sq, rsq = sq_b[bi], r_sq[bi]
                xm, rxm = xn_tm[bi], r_xntm[bi]
                stt, rst = st_b[bi], r_st[bi]
                S.op("SP", lambda s, xt=xt, tc=tc: s.dma_start(
                    out=xt[:], in_=x_in[tc * 128:(tc + 1) * 128, :]),
                    dma_into=rxt)
                S.op("DVE", lambda v, xt=xt, stt=stt: v.reduce_sum(
                    stt[:, 0:1], xt[:], axis=AX.X),
                    reads=[rxt], writes=[rst])
                S.op("DVE", lambda v, stt=stt: v.tensor_scalar(
                    stt[:, 1:2], stt[:, 0:1], -1.0 / Hd, None, ALU.mult),
                    reads=[rst], writes=[rst])
                S.op("ACT", lambda s, xt=xt, sq=sq, stt=stt: s.activation(
                    sq[:], xt[:], AF.Square, bias=stt[:, 1:2]),
                    reads=[rxt, rst], writes=[rsq])
                S.op("DVE", lambda v, sq=sq, stt=stt: v.reduce_sum(
                    stt[:, 2:3], sq[:], axis=AX.X),
                    reads=[rsq], writes=[rst])
                S.op("ACT", lambda s, stt=stt: s.activation(
                    stt[:, 3:4], stt[:, 2:3], AF.Sqrt,
                    scale=cst[:, 1:2], bias=cst[:, 0:1]),
                    reads=[rst, r_cst], writes=[rst])
                S.op("DVE", lambda v, stt=stt: v.reciprocal(
                    stt[:, 4:5], stt[:, 3:4]),
                    reads=[rst], writes=[rst])
                S.op("DVE", lambda v, stt=stt: v.tensor_tensor(
                    stt[:, 5:6], stt[:, 1:2], stt[:, 4:5], ALU.mult),
                    reads=[rst], writes=[rst])
                S.op("DVE", lambda v, xt=xt, xm=xm, stt=stt: v.tensor_scalar(
                    xm[:], xt[:], stt[:, 4:5], stt[:, 5:6],
                    ALU.mult, ALU.add),
                    reads=[rxt, rst], writes=[rxm])

            def ln_stage2(tc):
                bi = tc % 2
                xm, rxm = xn_tm[bi], r_xntm[bi]
                rblk = r_xnall[tc // 2]
                c0, c1 = tc * 128, (tc + 1) * 128
                for q4 in range(4):
                    bnk, rbk = nbank()
                    for j in range(4):
                        fc = q4 * 4 + j
                        S.op("PE", lambda t, xm=xm, fc=fc, j=j, b=bnk: t.transpose(
                            b[:, j * 128:(j + 1) * 128],
                            xm[:, fc * 128:(fc + 1) * 128], ident_f32[:]),
                            reads=[rxm, r_identf], writes=[rbk])
                    for j in range(4):
                        fc = q4 * 4 + j
                        S.op("DVE", lambda v, fc=fc, j=j, b=bnk, c0=c0, c1=c1:
                             v.tensor_copy(
                                 xn_all[:, fc, c0:c1], b[:, j * 128:(j + 1) * 128]),
                             reads=[rbk], writes=[rblk])
                S.op("SP", lambda s, tc=tc, c0=c0, c1=c1: s.dma_start(
                    out=xn_sc[:, :, c0:c1], in_=xn_all[:, :, c0:c1]),
                    reads=[rblk], dma_into=r_xnsc[tc // 2])

            NTC = TPC // 128

            HG = 4
            wkvg = sb0("wkvg", [128, HG, NFC, 256], BF16)
            kb_b = [sb0(f"kb{i}", [128, BLK], BF16) for i in range(3)]
            vb_b = [sb0(f"vb{i}", [128, BLK], BF16) for i in range(3)]
            ktm_b = [sb0(f"ktm{i}", [128, BLK], BF16) for i in range(2)]
            vtm_b = [sb0(f"vtm{i}", [128, BLK], BF16) for i in range(2)]

            r_wkvgh = [S.dres("wkvgA"), S.dres("wkvgB")]
            r_kb = [S.res(f"kb{i}") for i in range(3)]
            r_vb = [S.res(f"vb{i}") for i in range(3)]
            r_ktm = [S.res(f"ktm{i}") for i in range(2)]
            r_vtm = [S.res(f"vtm{i}") for i in range(2)]

            def p1_proj(blk, hh, si):
                rxb = r_xnall[blk]
                rwk = r_wkvgh[hh // 2]
                m0, m1 = blk * BLK, (blk + 1) * BLK
                par3 = si % 3
                kb, rkb = kb_b[par3], r_kb[par3]
                vb, rvb = vb_b[par3], r_vb[par3]
                bnk, rbk = nbank()
                for fc in range(NFC):
                    S.op("PE", lambda t, b=bnk, hh=hh, fc=fc, m0=m0, m1=m1:
                         t.matmul(b[:, 0:BLK], wkvg[:, hh, fc, 0:128],
                                  xn_all[:, fc, m0:m1],
                                  start=(fc == 0), stop=(fc == NFC - 1)),
                         reads=[rwk, rxb], writes=[rbk])
                S.op("ACT", lambda s, b=bnk, kb=kb: s.activation(
                    kb[:], b[:, 0:BLK], AF.Silu),
                    reads=[rbk], writes=[rkb])
                bnk2, rbk2 = nbank()
                for fc in range(NFC):
                    S.op("PE", lambda t, b=bnk2, hh=hh, fc=fc, m0=m0, m1=m1:
                         t.matmul(b[:, 0:BLK], wkvg[:, hh, fc, 128:256],
                                  xn_all[:, fc, m0:m1],
                                  start=(fc == 0), stop=(fc == NFC - 1)),
                         reads=[rwk, rxb], writes=[rbk2])
                S.op("ACT", lambda s, b=bnk2, vb=vb: s.activation(
                    vb[:], b[:, 0:BLK], AF.Silu),
                    reads=[rbk2], writes=[rvb])

            def p1_T(blk, hh, si):
                par3 = si % 3
                par = si % 2
                kb, rkb = kb_b[par3], r_kb[par3]
                vb, rvb = vb_b[par3], r_vb[par3]
                ptb, rptb = pt2[par], pt_res2[par]
                for j in range(2):
                    S.op("PE", lambda t, j=j, kb=kb, ptb=ptb: t.transpose(
                        ptb[:, j * 128:(j + 1) * 128],
                        kb[:, j * 128:(j + 1) * 128], ident_sb[:]),
                        reads=[rkb, r_ident], writes=[rptb])
                    S.op("PE", lambda t, j=j, vb=vb, ptb=ptb: t.transpose(
                        ptb[:, (2 + j) * 128:(3 + j) * 128],
                        vb[:, j * 128:(j + 1) * 128], ident_sb[:]),
                        reads=[rvb, r_ident], writes=[rptb])

            def p1_A(blk, hh, si, hg):
                h = hg * HG + hh
                par3 = si % 3
                par = si % 2
                kb, rkb = kb_b[par3], r_kb[par3]
                ktm, rktm = ktm_b[par], r_ktm[par]
                vtm, rvtm = vtm_b[par], r_vtm[par]
                ptb, rptb = pt2[par], pt_res2[par]
                cur, nxt = blk % 2, (blk + 1) % 2
                for j in range(2):
                    S.op("DVE", lambda v, h=h, j=j, ktm=ktm, ptb=ptb:
                         v.tensor_scalar(
                             ktm[:, j * 128:(j + 1) * 128],
                             ptb[:, j * 128:(j + 1) * 128],
                             kd_all[:, h, j:j + 1], None, ALU.mult),
                         reads=[rptb, r_kd[h]], writes=[rktm])
                    S.op("DVE", lambda v, j=j, vtm=vtm, ptb=ptb:
                         v.tensor_copy(
                             vtm[:, j * 128:(j + 1) * 128],
                             ptb[:, (2 + j) * 128:(3 + j) * 128]),
                         reads=[rptb], writes=[rvtm])
                bnk3, rbk3 = nbank()
                for j in range(2):
                    S.op("PE", lambda t, b=bnk3, j=j, ktm=ktm, vtm=vtm:
                         t.matmul(
                             b[:, 0:HD], ktm[:, j * 128:(j + 1) * 128],
                             vtm[:, j * 128:(j + 1) * 128],
                             start=(j == 0), stop=(j == 1)),
                         reads=[rktm, rvtm], writes=[rbk3])
                S.op("DVE", lambda v, b=bnk3, h=h, cur=cur, nxt=nxt:
                     v.scalar_tensor_tensor(
                         kv[nxt][:, h, :], kv[cur][:, h, :],
                         bd_bc[:, h:h + 1], b[:, 0:HD],
                         ALU.mult, ALU.add),
                     reads=[r_kv[cur][h], r_bdbc, rbk3],
                     writes=[r_kv[nxt][h]])
                for kind, buf, rbuf in ((0, kb, rkb), (1, ktm, rktm),
                                        (2, vtm, rvtm)):
                    S.op("SP", lambda s, kind=kind, buf=buf, h=h, blk=blk:
                         s.dma_start(
                             out=kvf_sc[:, kind, h, blk * BLK:(blk + 1) * BLK],
                             in_=buf[:]),
                         reads=[rbuf], dma_into=r_kvfsc)

            pend = []
            si_box = [0]

            def scan_step(blk, hh, hg):
                si = si_box[0]
                p1_proj(blk, hh, si)
                pend.append((blk, hh, si, hg))
                if len(pend) >= 2:
                    p1_T(pend[-2][0], pend[-2][1], pend[-2][2])
                if len(pend) >= 3:
                    b3, h3, s3, g3 = pend.pop(0)
                    p1_A(b3, h3, s3, g3)
                si_box[0] += 1

            def scan_flush():
                if pend:
                    p1_T(pend[-1][0], pend[-1][1], pend[-1][2])
                while pend:
                    b3, h3, s3, g3 = pend.pop(0)
                    p1_A(b3, h3, s3, g3)

            def wkvg_load(hg, half):
                h0 = hg * HG + half * 2
                S.op("POOL", lambda g, h0=h0, half=half: g.dma_start(
                    out=wkvg[:, half * 2:half * 2 + 2, :, :],
                    in_=wkv_d[:, h0:h0 + 2, :, :]),
                    dma_into=r_wkvgh[half])

            def after_step(blk, hh, hg):
                if blk == NBLK - 1 and hh % 2 == 1 and hg + 1 < NH // HG:
                    wkvg_load(hg + 1, hh // 2)

            wkvg_load(0, 0)
            wkvg_load(0, 1)
            for tc in range(NTC):
                ln_chain(tc)
                if tc >= 1:
                    ln_stage2(tc - 1)
                    if (tc - 1) % 2 == 1:
                        blk = (tc - 1) // 2
                        for hh in range(HG):
                            scan_step(blk, hh, 0)
                            after_step(blk, hh, 0)
            ln_stage2(NTC - 1)
            for hh in range(HG):
                scan_step(NBLK - 1, hh, 0)
                after_step(NBLK - 1, hh, 0)

            for hg in range(1, NH // HG):
                for blk in range(NBLK):
                    for hh in range(HG):
                        scan_step(blk, hh, hg)
                        after_step(blk, hh, hg)
            scan_flush()

        S.barrier()
        slot = [sb(f"slot{i}", [128, SLOTW], BF16) for i in range(NSLOT)]
        r_slot = [S.dres(f"slot{i}") for i in range(NSLOT)]
        r_wsbf = S.dres("wsbf")
        sall = sb("sall", [128, NH * HD], F32)
        r_sall = S.dres("sall")

        def emit_p2a():
            S.op("SP", lambda s: s.dma_start(
                out=cc_in[:, :].rearrange("p (h d) -> p h d", h=NH),
                in_=kv[0][:, :, :]),
                reads=[r_kv[0][h] for h in range(NH)], dma_into=r_ccin)
            S.op("POOL", lambda g: g.collective_compute(
                "AllGather", ALU.bypass, replica_groups=[list(range(NCORE))],
                ins=[cc_in.ap().opt()], outs=[cc_gather.ap().opt()]),
                reads=[r_ccin], dma_into=r_ccg, dma_inc=1)
            for h in range(NH):
                S.op("DVE", lambda v, h=h: v.memset(kv[0][:, h, :], 0.0),
                     writes=[r_kv[0][h]])

        def emit_p2b():
            for r in range(NCORE):
                S.op("SP", lambda s, r=r: s.dma_start(
                    out=sall[:], in_=cc_gather[r * 128:(r + 1) * 128, :]),
                    reads=[r_ccg], dma_into=r_sall)
                cur, nxt = r % 2, (r + 1) % 2
                for h in range(NH):
                    S.op("DVE", lambda v, r=r, h=h, cur=cur, nxt=nxt:
                         v.scalar_tensor_tensor(
                             kv[nxt][:, h, :], sall[:, h * HD:(h + 1) * HD],
                             coef_bc[:, h * NCORE + r: h * NCORE + r + 1],
                             kv[cur][:, h, :], ALU.mult, ALU.add),
                         reads=[r_sall, r_coefbc, r_kv[cur][h]],
                         writes=[r_kv[nxt][h]])
            for h in range(NH):
                S.op("ACT", lambda s, h=h: s.copy(kv_bf[0][:, h, :], kv[0][:, h, :]),
                     reads=[r_kv[0][h]], writes=[r_kvbf[0][h]])

        with ExitStack() as p3:
            def sb3(name, shape, dt):
                return p3.enter_context(nc.sbuf_tensor(name, shape, dt))

            xn_sl2 = [sb3(f"xn_sl{i}", [128, NFC, TT], BF16) for i in range(2)]
            acc = sb3("acc", [128, 4, Hd], F32)
            o_fm = sb3("o_fm", [128, NFC, TT], BF16)
            qf2 = [sb3(f"qf{i}", [128, TT], BF16) for i in range(2)]
            kvf_b = [sb3(f"kvf{i}", [128, 3, TT], BF16) for i in range(2)]
            kf2 = [kvf_b[i][:, 0, :] for i in range(2)]
            ktm32 = [kvf_b[i][:, 1, :] for i in range(2)]
            vtm32 = [kvf_b[i][:, 2, :] for i in range(2)]
            qkd2 = [sb3(f"qkd{i}", [128, 4, BLK], BF16) for i in range(2)]
            t12 = [sb3(f"t1{i}", [128, BLK], F32) for i in range(2)]
            osq = [sb3(f"osq{i}", [128, TT], BF16) for i in range(2)]
            g_tmp = sb3("g_tmp", [128, TT], BF16)
            t2 = sb3("t2", [128, TT], BF16)
            var_row = sb3("var_row", [1, TT], F32)
            rstd_bc = sb3("rstd_bc", [128, TT], BF16)
            ce = sb3("ce", [128, 4, MOE_E], F32)
            lg = sb3("lg", [128, 64], F32)
            tfm = [sb3(f"tfm{i}", [128, TT], BF16) for i in range(NIC)]
            hsb = sb3("hsb", [128, TT], BF16)

            r_xnsl = [S.dres(f"xnsl{i}") for i in range(2)]
            r_acc = [S.dres(f"acc{mc}") for mc in range(4)]
            r_ofm = [S.res(f"ofm{oc}") for oc in range(NFC)]
            r_qf = [S.res(f"qf{i}") for i in range(2)]
            r_kvf = [S.dres(f"kvf{i}") for i in range(2)]
            r_kf = r_kvf
            r_ktm3 = r_kvf
            r_vtm3 = r_kvf
            r_qkd = [S.res(f"qkd{i}") for i in range(2)]
            r_t1 = [S.res(f"t1{i}") for i in range(2)]
            r_osq = [S.res(f"osq{i}") for i in range(2)]
            r_gtmp = S.res("gtmp")
            r_t2 = S.res("t2")
            r_var = S.res("var")
            r_rstd = S.res("rstd")
            r_ce = [S.res(f"ce{mc}") for mc in range(4)]
            r_lg = S.res("lg")
            r_tfm = [S.res(f"tfm{i}") for i in range(NIC)]
            r_hsb = S.res("hsb")

            nxc = NEXP * (len(IC_GROUPS) * 2 + 4)
            table3 = table[nxc:] + table[:nxc]
            chunks_flat = table * (NTILE - 1) + table3
            emitted = [0]
            consumed = [0]
            ntab = len(table)

            def ensure_chunks(upto):
                while emitted[0] <= upto:
                    ci = emitted[0]
                    _, _, off, w = chunks_flat[ci]
                    i = ci % NSLOT
                    if ci < ntab:
                        S.op("POOL", lambda g, i=i, off=off, w=w: g.dma_start(
                            out=slot[i][:, 0:w], in_=wstream[:, off:off + w]),
                            dma_into=r_slot[i])
                        S.op("SP", lambda s, i=i, off=off, w=w: s.dma_start(
                            out=wstream_bf[:, off:off + w], in_=slot[i][:, 0:w]),
                            reads=[r_slot[i]], dma_into=r_wsbf)
                    else:
                        S.op("POOL", lambda g, i=i, off=off, w=w: g.dma_start(
                            out=slot[i][:, 0:w], in_=wstream_bf[:, off:off + w]),
                            reads=[r_wsbf], dma_into=r_slot[i])
                    emitted[0] += 1

            def next_slot(kind, meta, off, w):
                ci = consumed[0]
                ck, cm, co, cw = chunks_flat[ci]
                assert (ck, cm, co, cw) == (kind, meta, off, w), (ci, ck, kind)
                ensure_chunks(ci)
                consumed[0] += 1
                i = ci % NSLOT
                return slot[i], r_slot[i]

            pending_xout = [None]

            def emit_xout(t0x, last=False):
                for mc in range(4):
                    S.op("SP", lambda s, t0x=t0x, mc=mc: s.dma_start(
                        out=sall[:],
                        in_=x_in[t0x + mc * 128: t0x + (mc + 1) * 128, :]),
                        dma_into=r_sall)
                    S.op("DVE", lambda v, mc=mc: v.tensor_tensor(
                        acc[:, mc, :], acc[:, mc, :], sall[:], ALU.add),
                        reads=[r_sall, r_acc[mc]], writes=[r_acc[mc]])
                    S.op("POOL" if last else "SP",
                         lambda s, t0x=t0x, mc=mc: s.dma_start(
                             out=out_d[t0x + mc * 128: t0x + (mc + 1) * 128, :],
                             in_=acc[:, mc, :]),
                         reads=[r_acc[mc]], dma_into=r_out2 if last else r_out)

            for tile in range(NTILE):
                t0 = tile * TT
                cur0 = 0
                xn_sl = xn_sl2[tile % 2]
                r_xnsl_t = r_xnsl[tile % 2]
                attn_first = (tile == NTILE - 1)
                ttab = table3 if attn_first else table

                def load_xnsl(tl):
                    S.op("SP", lambda s, tl=tl: s.dma_start(
                        out=xn_sl2[tl % 2][:],
                        in_=xn_sc[:, :, tl * TT:(tl + 1) * TT]),
                        reads=[r_xnsc[2 * tl], r_xnsc[2 * tl + 1]],
                        dma_into=r_xnsl[tl % 2])

                if tile == 0:
                    load_xnsl(0)

                tib = [0]

                def sec_logits():
                    for mc in range(4):
                        bnk, rbk = nbank()
                        for fc in range(NFC):
                            S.op("PE", lambda t, b=bnk, fc=fc, mc=mc, xn_sl=xn_sl:
                                 t.matmul(
                                     b[:, 0:MOE_E],
                                     xn_sl[:, fc, mc * 128:(mc + 1) * 128],
                                     gw_sb[:, fc, :],
                                     start=(fc == 0), stop=(fc == NFC - 1)),
                                 reads=[r_xnsl_t, r_gw], writes=[rbk])
                        E8 = MOE_E
                        c0 = mc * E8
                        S.op("DVE", lambda v, b=bnk: v.reduce_max(
                            lg[:, 32:33], b[:, 0:E8], axis=AX.X),
                            reads=[rbk], writes=[r_lg])
                        S.op("DVE", lambda v, b=bnk, c0=c0: v.tensor_scalar(
                            lg[:, c0:c0 + E8], b[:, 0:E8], lg[:, 32:33], None,
                            ALU.subtract),
                            reads=[rbk, r_lg], writes=[r_lg])
                        S.op("DVE", lambda v, c0=c0, mc=mc: v.tensor_scalar(
                            ce[:, mc, :], lg[:, c0:c0 + E8], 0.0, None, ALU.is_ge),
                            reads=[r_lg], writes=[r_ce[mc]])
                        S.op("DVE", lambda v, c0=c0, mc=mc: v.scalar_tensor_tensor(
                            lg[:, 34:34 + E8], ce[:, mc, :], -1.0e9,
                            lg[:, c0:c0 + E8], ALU.mult, ALU.add),
                            reads=[r_ce[mc], r_lg], writes=[r_lg])
                        S.op("DVE", lambda v: v.reduce_max(
                            lg[:, 42:43], lg[:, 34:34 + MOE_E], axis=AX.X),
                            reads=[r_lg], writes=[r_lg])
                        S.op("ACT", lambda s, c0=c0: s.activation(
                            lg[:, 43:43 + MOE_E], lg[:, c0:c0 + MOE_E], AF.Exp),
                            reads=[r_lg], writes=[r_lg])
                        S.op("ACT", lambda s: s.activation(
                            lg[:, 51:52], lg[:, 42:43], AF.Exp),
                            reads=[r_lg], writes=[r_lg])
                        S.op("DVE", lambda v: v.tensor_scalar(
                            lg[:, 52:53], lg[:, 51:52], 1.0, None, ALU.add),
                            reads=[r_lg], writes=[r_lg])
                        S.op("DVE", lambda v: v.reciprocal(
                            lg[:, 53:54], lg[:, 52:53]),
                            reads=[r_lg], writes=[r_lg])
                        S.op("DVE", lambda v, c0=c0: v.tensor_scalar(
                            lg[:, 54:54 + MOE_E], lg[:, c0:c0 + MOE_E],
                            lg[:, 42:43], None, ALU.is_ge),
                            reads=[r_lg], writes=[r_lg])
                        S.op("DVE", lambda v: v.tensor_tensor(
                            lg[:, 43:43 + MOE_E], lg[:, 43:43 + MOE_E],
                            lg[:, 54:54 + MOE_E], ALU.mult),
                            reads=[r_lg], writes=[r_lg])
                        S.op("DVE", lambda v, mc=mc: v.tensor_scalar(
                            ce[:, mc, :], lg[:, 43:43 + MOE_E], lg[:, 53:54],
                            None, ALU.mult),
                            reads=[r_lg], writes=[r_ce[mc]])

                def sec_experts():
                    for e in range(NEXP):
                        if tile == 0 and e == 1:
                            emit_p2a()
                        if tile == 0 and e == 6:
                            emit_p2b()
                        for gi, (i0, i1) in enumerate(IC_GROUPS):
                            kind, meta, off, w = ttab[tib[0]]; tib[0] += 1
                            assert kind == "w1" and meta == (e, gi)
                            sl1, rsl1 = next_slot(kind, meta, off, w)
                            sv1 = sl1[:, 0:w].rearrange("p (c w) -> p c w", c=NFC)
                            kind, meta, off, w = ttab[tib[0]]; tib[0] += 1
                            assert kind == "w3" and meta == (e, gi)
                            sl3, rsl3 = next_slot(kind, meta, off, w)
                            sv3 = sl3[:, 0:w].rearrange("p (c w) -> p c w", c=NFC)
                            for ic in range(i0, i1):
                                jj = ic - i0
                                bnh, rbh = nbank()
                                for fc in range(NFC):
                                    S.op("PE", lambda t, b=bnh, fc=fc, jj=jj, sv1=sv1,
                                         xn_sl=xn_sl:
                                         t.matmul(b[:], sv1[:, fc, jj * 128:(jj + 1) * 128],
                                                  xn_sl[:, fc, :],
                                                  start=(fc == 0), stop=(fc == NFC - 1)),
                                         reads=[rsl1, r_xnsl_t], writes=[rbh])
                                S.op("ACT", lambda s, b=bnh: s.activation(
                                    hsb[:], b[:], AF.Silu),
                                    reads=[rbh], writes=[r_hsb])
                                bng, rbg = nbank()
                                for fc in range(NFC):
                                    S.op("PE", lambda t, b=bng, fc=fc, jj=jj, sv3=sv3,
                                         xn_sl=xn_sl:
                                         t.matmul(b[:], sv3[:, fc, jj * 128:(jj + 1) * 128],
                                                  xn_sl[:, fc, :],
                                                  start=(fc == 0), stop=(fc == NFC - 1)),
                                         reads=[rsl3, r_xnsl_t], writes=[rbg])
                                S.op("DVE", lambda v, b=bng, ic=ic: v.tensor_tensor(
                                    tfm[ic][:], hsb[:], b[:], ALU.mult),
                                    reads=[r_hsb, rbg], writes=[r_tfm[ic]])
                        for fb in range(4):
                            kind, meta, off, w = ttab[tib[0]]; tib[0] += 1
                            assert kind == "w2" and meta == (e, fb)
                            sl2, rsl2 = next_slot(kind, meta, off, w)
                            sv2 = sl2[:, 0:w].rearrange("p (c w) -> p c w", c=NIC)
                            for mc in range(4):
                                bny, rby = nbank()
                                for ic in range(NIC):
                                    S.op("PE", lambda t, b=bny, ic=ic, mc=mc, sv2=sv2:
                                         t.matmul(b[:],
                                                  tfm[ic][:, mc * 128:(mc + 1) * 128],
                                                  sv2[:, ic, :],
                                                  start=(ic == 0), stop=(ic == NIC - 1)),
                                         reads=[r_tfm[ic], rsl2], writes=[rby])
                                if e == 0 and not attn_first:
                                    S.op("DVE", lambda v, b=bny, mc=mc, fb=fb:
                                         v.tensor_scalar(
                                             acc[:, mc, fb * 512:(fb + 1) * 512],
                                             b[:], ce[:, mc, 0:1], None, ALU.mult),
                                         reads=[rby, r_ce[mc]], writes=[r_acc[mc]])
                                elif e < MOE_E:
                                    S.op("DVE", lambda v, b=bny, mc=mc, fb=fb, e=e:
                                         v.scalar_tensor_tensor(
                                             acc[:, mc, fb * 512:(fb + 1) * 512],
                                             b[:], ce[:, mc, e:e + 1],
                                             acc[:, mc, fb * 512:(fb + 1) * 512],
                                             ALU.mult, ALU.add),
                                         reads=[rby, r_ce[mc], r_acc[mc]],
                                         writes=[r_acc[mc]])
                                else:
                                    S.op("DVE", lambda v, b=bny, mc=mc, fb=fb:
                                         v.tensor_tensor(
                                             acc[:, mc, fb * 512:(fb + 1) * 512],
                                             acc[:, mc, fb * 512:(fb + 1) * 512],
                                             b[:], ALU.add),
                                         reads=[rby, r_acc[mc]], writes=[r_acc[mc]])

                def sec_attn():
                    def at_proj(h):
                        kind, meta, off, w = ttab[tib[0]]; tib[0] += 1
                        assert kind == "wq" and meta == h
                        sl, rsl = next_slot(kind, meta, off, w)
                        slv = sl[:, 0:w].rearrange("p (c w) -> p c w", c=NFC)
                        par = h % 2
                        S.op("SP", lambda s, par=par, h=h, t0=t0: s.dma_start(
                            out=kvf_b[par][:, :, :],
                            in_=kvf_sc[:, :, h, t0:t0 + TT]),
                            reads=[r_kvfsc], dma_into=r_kvf[par])
                        dst, rdst = qf2[par], r_qf[par]
                        bnk, rbk = nbank()
                        for fc in range(NFC):
                            S.op("PE", lambda t, b=bnk, fc=fc, slv=slv,
                                 xn_sl=xn_sl:
                                 t.matmul(b[:], slv[:, fc, 0:128],
                                          xn_sl[:, fc, :],
                                          start=(fc == 0), stop=(fc == NFC - 1)),
                                 reads=[rsl, r_xnsl_t], writes=[rbk])
                        S.op("ACT", lambda s, b=bnk, dst=dst: s.activation(
                            dst[:], b[:], AF.Silu),
                            reads=[rbk], writes=[rdst])

                    def at_tail_a(h):
                        par = h % 2
                        qf, kf = qf2[par], kf2[par]
                        ktm3, vtm3 = ktm32[par], vtm32[par]
                        qkd, t1 = qkd2[par], t12[par]
                        rqf, rkf = r_qf[par], r_kf[par]
                        rktm3, rvtm3 = r_ktm3[par], r_vtm3[par]
                        rqkd, rt1 = r_qkd[par], r_t1[par]
                        bni, rbi = nbank()
                        S.op("PE", lambda t, b=bni, h=h, qf=qf:
                             t.matmul(
                                 b[:, 0:BLK], kv_bf[0][:, h, :],
                                 qf[:, 0:BLK], start=True, stop=True),
                             reads=[r_kvbf[0][h], rqf], writes=[rbi])
                        for b_ in range(2):
                            m0 = b_ * BLK
                            for nch in range(2):
                                bnq, rbq = nbank()
                                S.op("PE", lambda t, b=bnq, nch=nch, m0=m0, kf=kf,
                                     qf=qf: t.matmul(
                                         b[:, 0:BLK],
                                         kf[:, m0 + nch * 128:m0 + (nch + 1) * 128],
                                         qf[:, m0:m0 + BLK], start=True, stop=True),
                                     reads=[rkf, rqf], writes=[rbq])
                                S.op("DVE", lambda v, b=bnq, h=h, nch=nch, b_=b_,
                                     qkd=qkd: v.tensor_tensor(
                                         qkd[:, b_ * 2 + nch, :], b[:, 0:BLK],
                                         decayT[:, h, nch * BLK:(nch + 1) * BLK],
                                         ALU.mult),
                                     reads=[rbq, r_decayT[h]], writes=[rqkd])
                        S.op("DVE", lambda v, b=bni, h=h, t1=t1: v.tensor_tensor(
                            t1[:], b[:, 0:BLK], qd_bc[:, h, :], ALU.mult),
                            reads=[rbi, r_qdbc[h]], writes=[rt1])
                        bna, rba = nbank()
                        for nch in range(2):
                            S.op("PE", lambda t, b=bna, nch=nch, vtm3=vtm3,
                                 qkd=qkd: t.matmul(
                                     b[:, 0:BLK],
                                     vtm3[:, nch * 128:(nch + 1) * 128],
                                     qkd[:, nch, :],
                                     start=(nch == 0), stop=(nch == 1)),
                                 reads=[rvtm3, rqkd], writes=[rba])
                        S.op("DVE", lambda v, b=bna, h=h, t1=t1:
                             v.tensor_tensor(
                                 o_fm[:, h, 0:BLK], t1[:], b[:, 0:BLK],
                                 ALU.add),
                             reads=[rt1, rba], writes=[r_ofm[h]])
                        for b_ in range(2):
                            m0 = b_ * BLK
                            cur, nxt = b_ % 2, 1 - b_ % 2
                            bnA, rbA = nbank()
                            for nch in range(2):
                                S.op("PE", lambda t, b=bnA, nch=nch, m0=m0,
                                     ktm3=ktm3, vtm3=vtm3: t.matmul(
                                         b[:, 0:HD],
                                         ktm3[:, m0 + nch * 128:m0 + (nch + 1) * 128],
                                         vtm3[:, m0 + nch * 128:m0 + (nch + 1) * 128],
                                         start=(nch == 0), stop=(nch == 1)),
                                     reads=[rktm3, rvtm3], writes=[rbA])
                            S.op("DVE", lambda v, b=bnA, h=h, cur=cur, nxt=nxt:
                                 v.scalar_tensor_tensor(
                                     kv[nxt][:, h, :], kv[cur][:, h, :],
                                     bd_bc[:, h:h + 1], b[:, 0:HD],
                                     ALU.mult, ALU.add),
                                 reads=[r_kv[cur][h], r_bdbc, rbA],
                                 writes=[r_kv[nxt][h]])
                            S.op("ACT", lambda s, h=h, nxt=nxt: s.copy(
                                kv_bf[nxt][:, h, :], kv[nxt][:, h, :]),
                                reads=[r_kv[nxt][h]], writes=[r_kvbf[nxt][h]])
                        bna1, rba1 = nbank()
                        for nch in range(2):
                            S.op("PE", lambda t, b=bna1, nch=nch, vtm3=vtm3,
                                 qkd=qkd: t.matmul(
                                     b[:, 0:BLK],
                                     vtm3[:, BLK + nch * 128:BLK + (nch + 1) * 128],
                                     qkd[:, 2 + nch, :],
                                     start=(nch == 0), stop=(nch == 1)),
                                 reads=[rvtm3, rqkd], writes=[rba1])
                        return bna1, rba1

                    def at_tail_b(h, bna1, rba1):
                        par = h % 2
                        qf, t1 = qf2[par], t12[par]
                        rqf, rt1 = r_qf[par], r_t1[par]
                        bnj, rbj = nbank()
                        S.op("PE", lambda t, b=bnj, h=h, qf=qf:
                             t.matmul(
                                 b[:, 0:BLK], kv_bf[1][:, h, :],
                                 qf[:, BLK:2 * BLK], start=True, stop=True),
                             reads=[r_kvbf[1][h], rqf], writes=[rbj])
                        S.op("DVE", lambda v, b=bnj, h=h, t1=t1: v.tensor_tensor(
                            t1[:], b[:, 0:BLK], qd_bc[:, h, :], ALU.mult),
                            reads=[rbj, r_qdbc[h]], writes=[rt1])
                        S.op("DVE", lambda v, b=bna1, h=h, t1=t1:
                             v.tensor_tensor(
                                 o_fm[:, h, BLK:2 * BLK], t1[:], b[:, 0:BLK],
                                 ALU.add),
                             reads=[rt1, rba1], writes=[r_ofm[h]])

                    def var_step(h):
                        ob = osq[h % 2]
                        rob = r_osq[h % 2]
                        S.op("ACT", lambda s, ob=ob, h=h: s.activation(
                            ob[:], o_fm[:, h, :], AF.Square),
                            reads=[r_ofm[h]], writes=[rob])
                        S.op("PE", lambda t, ob=ob, h=h: t.matmul(
                            pb[5][0:1, 0:TT], ones_colb[:], ob[:],
                            start=(h == 0), stop=(h == NH - 1)),
                            reads=[rob, r_ones], writes=[pb_res[5]])

                    pa = [None]
                    for h in range(NH + 1):
                        if h >= 1:
                            pa[0] = at_tail_a(h - 1)
                        if h < NH:
                            at_proj(h)
                        if h >= 1:
                            at_tail_b(h - 1, *pa[0])
                            var_step(h - 1)

                def sec_rms():
                    S.op("ACT", lambda s: s.activation(
                        var_row[0:1, :], pb[5][0:1, 0:TT], AF.Sqrt,
                        scale=cst[0:1, 1:2], bias=cst[0:1, 2:3]),
                        reads=[pb_res[5]], writes=[r_var])
                    S.op("DVE", lambda v: v.reciprocal(var_row[0:1, :], var_row[0:1, :]),
                         reads=[r_var], writes=[r_var])
                    bnr, rbr = nbank()
                    S.op("PE", lambda t, b=bnr: t.matmul(
                        b[:, 0:TT], ones_rowf[:], var_row[0:1, :],
                        start=True, stop=True),
                        reads=[r_ones, r_var], writes=[rbr])
                    S.op("DVE", lambda v, b=bnr: v.tensor_copy(rstd_bc[:], b[:, 0:TT]),
                         reads=[rbr], writes=[r_rstd])

                def sec_gate():
                    for g2 in range(8):
                        kind, meta, off, w = ttab[tib[0]]; tib[0] += 1
                        assert kind == "wg" and meta == g2
                        sl, rsl = next_slot(kind, meta, off, w)
                        slv = sl[:, 0:w].rearrange("p (c w) -> p c w", c=NFC)
                        for o2 in range(2):
                            oc = g2 * 2 + o2
                            bnk, rbk = nbank()
                            for fc in range(NFC):
                                S.op("PE", lambda t, b=bnk, fc=fc, o2=o2, slv=slv,
                                     xn_sl=xn_sl:
                                     t.matmul(b[:], slv[:, fc, o2 * 128:(o2 + 1) * 128],
                                              xn_sl[:, fc, :],
                                              start=(fc == 0), stop=(fc == NFC - 1)),
                                     reads=[rsl, r_xnsl_t], writes=[rbk])
                            S.op("ACT", lambda s, b=bnk: s.activation(
                                g_tmp[:], b[:], AF.Sigmoid),
                                reads=[rbk], writes=[r_gtmp])
                            S.op("DVE", lambda v, oc=oc: v.tensor_tensor(
                                t2[:], g_tmp[:], o_fm[:, oc, :], ALU.mult),
                                reads=[r_gtmp, r_ofm[oc]], writes=[r_t2])
                            S.op("DVE", lambda v, oc=oc: v.scalar_tensor_tensor(
                                o_fm[:, oc, :], t2[:], norm_sb[:, oc:oc + 1],
                                rstd_bc[:], ALU.mult, ALU.mult),
                                reads=[r_t2, r_norm, r_rstd], writes=[r_ofm[oc]])

                def sec_attny():
                    for fq in range(8):
                        kind, meta, off, w = ttab[tib[0]]; tib[0] += 1
                        assert kind == "wo" and meta == fq
                        sl, rsl = next_slot(kind, meta, off, w)
                        slv = sl[:, 0:w].rearrange("p (c w) -> p c w", c=NFC)
                        for mc in range(4):
                            bnk, rbk = nbank()
                            for oc in range(NFC):
                                S.op("PE", lambda t, b=bnk, oc=oc, mc=mc, slv=slv:
                                     t.matmul(b[:, 0:256],
                                              o_fm[:, oc, mc * 128:(mc + 1) * 128],
                                              slv[:, oc, :],
                                              start=(oc == 0), stop=(oc == NFC - 1)),
                                     reads=[r_ofm[oc], rsl], writes=[rbk])
                            if attn_first:
                                S.op("DVE", lambda v, b=bnk, mc=mc, fq=fq:
                                     v.tensor_copy(
                                         acc[:, mc, fq * 256:(fq + 1) * 256],
                                         b[:, 0:256]),
                                     reads=[rbk], writes=[r_acc[mc]])
                            else:
                                S.op("DVE", lambda v, b=bnk, mc=mc, fq=fq:
                                     v.tensor_tensor(
                                         acc[:, mc, fq * 256:(fq + 1) * 256],
                                         acc[:, mc, fq * 256:(fq + 1) * 256],
                                         b[:, 0:256], ALU.add),
                                     reads=[rbk, r_acc[mc]], writes=[r_acc[mc]])


                sec_logits()
                if attn_first:
                    sec_attn()
                    sec_rms()
                    sec_gate()
                    sec_attny()
                    sec_experts()
                else:
                    sec_experts()
                    sec_attn()
                    if tile + 1 < NTILE:
                        load_xnsl(tile + 1)
                    sec_rms()
                    sec_gate()
                    sec_attny()

                ensure_chunks(min(consumed[0] + NSLOT - 1, len(chunks_flat) - 1))
                emit_xout(t0, last=attn_first)

            S.op("SP", None, reads=[r_out, r_out2])

        S.finalize()
        with ExitStack() as semctx:
            semh = {}
            for e in _ENGS:
                semh["E:" + e] = semctx.enter_context(nc.semaphore("sem_" + e))
            for name in S.dma_sem_names:
                semh[name] = semctx.enter_context(
                    nc.semaphore(name.replace(":", "_")))
            with nc.Block() as block:
                S.emit(block, semh)

    return nc


_NC_CACHE = {}


def kernel(**inputs):
    in_maps, table = _prep(inputs)
    key = "main"
    if key not in _NC_CACHE:
        _NC_CACHE[key] = _build_nc(table)
    nc = _NC_CACHE[key]
    res = run_bass_kernel_spmd(nc, in_maps, core_ids=list(range(NCORE)))
    out = np.concatenate([res.results[c]["out"] for c in range(NCORE)], axis=0)
    return out.reshape(B, N, Hd)
